# version 10
# baseline (speedup 1.0000x reference)
import contextlib
import numpy as np
import ml_dtypes
import concourse.bass as bass
import concourse.mybir as mybir
from concourse.bass_utils import run_bass_kernel_spmd
from concourse.bass import IndirectOffsetOnAxis

F32 = mybir.dt.float32
BF16 = mybir.dt.bfloat16
F16 = mybir.dt.float16
I32 = mybir.dt.int32
U16 = mybir.dt.uint16
AF = mybir.ActivationFunctionType
ALU = mybir.AluOpType
AX = mybir.AxisListType
bf = ml_dtypes.bfloat16

D = 1024
HID = 768
NE = 64
CAP = 256
NCORE = 8
SEG = 2048
TOK = 2 * SEG
WIN = 16384
EPC = NE // NCORE
NSLOT = EPC * CAP
NH = 16
DH = 64
M = 512
TBL_ROWS = WIN + 128
TRASH_ROW = WIN + 1
EO_PAD = 16
EO_ROWS = NSLOT + EO_PAD
GUW = 2 * HID
WROW = 8 * GUW + 6 * D
STAGE = 99


def _p4g(n):
    d = np.arange(n)
    return (d // 128) * 128 + (d % 128 % 4) * 32 + (d % 128) // 4


def _inv(p):
    q = np.empty_like(p)
    q[p] = np.arange(len(p))
    return q


PD = _p4g(D)
PI = _p4g(HID)
INV_PD = _inv(PD)
INV_PI = _inv(PI)


def _pack_u16(packed_i32):
    b = np.ascontiguousarray(packed_i32.astype(np.int32)).view(np.uint8)
    b = np.ascontiguousarray(b.reshape(*packed_i32.shape, 4)[..., 0])
    return b.view(np.uint16)


def _host_prep(inputs):
    x = inputs["L0_attn_x"]
    wqkvT = np.ascontiguousarray(inputs["L0_attn_w_in"].T[INV_PD, :], np.float32)
    woutT = np.ascontiguousarray(inputs["L0_attn_w_out"].T[:, INV_PD], np.float32)
    gateT = np.ascontiguousarray(
        inputs["L0_ffn_gate_w"].T[INV_PD, :].astype(np.float32).reshape(8, 128, NE)
        .transpose(1, 0, 2))
    b_out_p = np.ascontiguousarray(
        inputs["L0_attn_b_out"][INV_PD].astype(np.float32).reshape(8, 128).T)
    b_in_c = np.ascontiguousarray(
        inputs["L0_attn_b_in"].astype(np.float32).reshape(24, 128).T)

    gp = _pack_u16(inputs["L0_ffn_gate_packed"])[:, INV_PI, :]
    up = _pack_u16(inputs["L0_ffn_up_packed"])[:, INV_PI, :]
    gup = np.concatenate([gp, up], axis=1)
    gup_sc = np.concatenate(
        [inputs["L0_ffn_gate_scales"][:, INV_PI, :],
         inputs["L0_ffn_up_scales"][:, INV_PI, :]], axis=1).astype(np.float32)
    dn = _pack_u16(inputs["L0_ffn_down_packed"])[:, INV_PD, :]
    dn_sc = inputs["L0_ffn_down_scales"][:, INV_PD, :].astype(np.float32)

    shgup = np.concatenate([_pack_u16(inputs["L0_ffn_sh_gate_packed"])[INV_PI, :],
                            _pack_u16(inputs["L0_ffn_sh_up_packed"])[INV_PI, :]], axis=0)
    shgup_sc = np.concatenate(
        [inputs["L0_ffn_sh_gate_scales"][INV_PI, :],
         inputs["L0_ffn_sh_up_scales"][INV_PI, :]], axis=0).astype(np.float32)
    shdn = _pack_u16(inputs["L0_ffn_sh_down_packed"])[INV_PD, :]
    shdn_sc = inputs["L0_ffn_sh_down_scales"][INV_PD, :].astype(np.float32)

    p128 = np.arange(128, dtype=np.int32)
    xr = x.reshape(8, 2, SEG, D)
    maps = []
    for c in range(NCORE):
        maps.append({
            "xw": np.ascontiguousarray(xr[c // 2, c % 2], np.float32),
            "xt": np.ascontiguousarray(xr[c // 2 + 4, c % 2], np.float32),
            "wqkvT": wqkvT, "woutT": woutT, "gateT": gateT,
            "b_out_p": b_out_p, "b_in_c": b_in_c,
            "gup_pk": np.ascontiguousarray(gup[EPC * c:EPC * (c + 1)]),
            "gup_sc": np.ascontiguousarray(gup_sc[EPC * c:EPC * (c + 1)]),
            "dn_pk": np.ascontiguousarray(dn[EPC * c:EPC * (c + 1)]),
            "dn_sc": np.ascontiguousarray(dn_sc[EPC * c:EPC * (c + 1)]),
            "sh_gup_pk": shgup, "sh_gup_sc": shgup_sc,
            "sh_dn_pk": shdn, "sh_dn_sc": shdn_sc,
            "tokbase": (p128.astype(np.float32) + 2048.0 * c).reshape(128, 1),
            "slrow": (p128 * 128 + 16 * c).astype(np.int32).reshape(128, 1),
            "maskpre": np.repeat((np.arange(8) < c).astype(np.float32).reshape(1, 8),
                                 64, axis=0),
        })
    return maps


class _B:
    def __init__(self, nc, es):
        self.nc = nc
        self.v = {}
        self.h = {}
        for s in ["ld", "gq", "sq", "cc", "mm", "ve", "sc", "gp"]:
            self.h[s] = es.enter_context(nc.semaphore(s))
            self.v[s] = 0

    def w(self, eng, *sems):
        for s in sems:
            if self.v[s] > 0:
                eng.wait_ge(self.h[s], self.v[s])

    def i(self, ins, sem, k=1):
        ins.then_inc(self.h[sem], k)
        self.v[sem] += k
        return ins


def _build(stage=None, debug=False, ffnsub=99):
    if stage is None:
        stage = STAGE
    nc = bass.Bass("TRN2", target_bir_lowering=False)
    es = contextlib.ExitStack()
    B = _B(nc, es)

    A = {}
    for name, shp, dt in [
        ("xw", [SEG, D], F32), ("xt", [SEG, D], F32),
        ("wqkvT", [D, 3 * D], F32), ("woutT", [D, D], F32), ("gateT", [128, 8, NE], F32),
        ("b_out_p", [128, 8], F32), ("b_in_c", [128, 24], F32),
        ("gup_pk", [EPC, GUW, 256], U16), ("gup_sc", [EPC, GUW, 8], F32),
        ("dn_pk", [EPC, D, 192], U16), ("dn_sc", [EPC, D, 6], F32),
        ("sh_gup_pk", [GUW, 256], U16), ("sh_gup_sc", [GUW, 8], F32),
        ("sh_dn_pk", [D, 192], U16), ("sh_dn_sc", [D, 6], F32),
        ("tokbase", [128, 1], F32), ("slrow", [128, 1], I32),
        ("maskpre", [64, 8], F32),
    ]:
        A[name] = nc.dram_tensor(name, shp, dt, kind="ExternalInput").ap()
    out_w = nc.dram_tensor("out_w", [SEG, D], F32, kind="ExternalOutput").ap()
    out_t = nc.dram_tensor("out_t", [SEG, D], F32, kind="ExternalOutput").ap()
    dbg = nc.dram_tensor("dbg", [128, 4096], F32, kind="ExternalOutput").ap() if debug else None

    DR = {}
    for name, shp, dt, kw in [
        ("wstage", [EPC, 128, WROW], BF16, {}),
        ("shstage", [128, WROW], BF16, {}),
        ("attnw", [128, 2 * 16384], BF16, {}),
        ("bounce", [TOK, D], BF16, {}),
        ("lnw_in", [SEG + EO_PAD, D], BF16, {}),

        ("cnt_in", [64, 1], F32, {}),

        ("tbl_in", [TBL_ROWS, 2], F32, {}),
        ("tblsum", [WIN, 2], F32, {}),
        ("eo_in", [EO_ROWS, D], BF16, {}),
        ("sharena", [NCORE * (SEG + EO_PAD) * D // 2 + NCORE * 64
                     + NCORE * EO_ROWS * D // 2 + NCORE * TBL_ROWS * 2], F32,
         {"addr_space": "Shared"}),
    ]:
        DR[name] = nc.dram_tensor(name, shp, dt, **kw)
    _sa = DR["sharena"].ap()
    _o1 = NCORE * (SEG + EO_PAD) * D // 2
    _o2 = _o1 + NCORE * 64
    _o3 = _o2 + NCORE * EO_ROWS * D // 2
    VIEWS = {
        "lnw_out": _sa[0:_o1].bitcast(BF16).rearrange("(r d) -> r d", d=D),
        "cnt_out": _sa[_o1:_o2].rearrange("(r c) -> r c", c=1),
        "eo_out": _sa[_o2:_o3].bitcast(BF16).rearrange("(r d) -> r d", d=D),
        "gall": DR["sharena"].ap().bitcast(BF16).rearrange("(r d) -> r d", d=D),
        "eo_row0": _o2 * 2 // D,
        "tbl_ag": _sa[_o3:_o3 + NCORE * TBL_ROWS * 2],
        "tbl_rk": (lambda r, _sa=_sa, _o3=_o3:
                   _sa[_o3 + r * TBL_ROWS * 2:_o3 + r * TBL_ROWS * 2 + WIN * 2]
                   .rearrange("(p a) -> p a", p=128)),
    }

    S = {}
    def tile(name, shape, dt):
        S[name] = es.enter_context(nc.sbuf_tensor(name, shape, dt))

    for t in range(8):
        tile(f"res{t}", [128, TOK], BF16)
    for t in range(8):
        tile(f"lnc{t}", [128, 512], BF16)
    tile("ones", [128, 128], BF16)
    tile("identB", [128, 128], BF16)
    tile("identF", [128, 128], F32)
    tile("iotaE", [128, 64], F32)
    tile("cmask", [128, 4, M], BF16)
    tile("gateTs", [128, 8, NE], BF16)
    tile("binS", [128, 24], F32)
    tile("boutS", [128, 8], F32)
    tile("slrows", [128, 1], I32)
    tile("tokb", [128, 1], F32)
    tile("trashp", [128, 1], F32)
    tile("eps1", [128, 1], F32)
    tile("maskp", [64, 8], F32)
    tile("wbuf", [128, WROW], BF16)
    tile("pk", [128, 256], U16)
    tile("pksc", [128, 8], F32)
    tile("bits", [128, 1024], U16)
    tile("bita", [128, 256], U16)
    tile("bitb", [128, 256], U16)
    tile("wnat", [128, 1024], BF16)
    tile("h6", [128, 6, 512], BF16)
    tile("gb", [128, 2, D], BF16)
    tile("gbT", [128, 8, 256], BF16)
    tile("eo", [128, D], BF16)
    tile("tslice", [128, 32], F32)
    tile("tbb", [128, 2, 256], F32)
    tile("tidx", [128, 16], I32)
    tile("twgt", [128, 16], F32)
    tile("logit", [128, 16, NE], F32)
    tile("mx8", [128, 16, 8], F32)
    tile("wj4", [128, 16, 4], F32)
    tile("amask", [128, 64], BF16)
    tile("scanb", [64, SEG], F32)
    tile("cnt8", [64, 8], F32)
    tile("base", [64, 1], F32)
    tile("post", [128, 64], F32)
    tile("sig", [128, 16, 4], I32)
    tile("scat4", [128, 4, 2], F32)
    tile("i32a", [128, 64], I32)
    tile("i32b", [128, 64], I32)
    tile("i32c", [128, 64], I32)
    tile("qkv", [128, 24, M], BF16)
    tile("pt", [128, 4, M], BF16)
    tile("vt", [128, 4, DH + 1], BF16)
    tile("otm", [128, 4, DH], BF16)
    tile("ofm", [128, 8, M], BF16)
    tile("xs", [128, D], BF16)
    tile("t32a", [128, 512], F32)
    tile("t32b", [128, 512], F32)
    tile("t32c", [128, 512], F32)
    tile("t32d", [128, 512], F32)
    tile("tb16", [128, 1024], BF16)
    tile("tb16b", [128, 1024], BF16)
    tile("f32x", [128, 1024], F32)

    PS = {}
    def ptile(name, shape, dt):
        PS[name] = es.enter_context(nc.psum_tensor(name, shape, dt))
    for n in ["psA", "psB", "psC", "psD", "psE", "psF"]:
        ptile(n, [128, 512], F32)
    ptile("ptrA", [128, 1024], BF16)
    ptile("ptrB", [128, 1024], BF16)

    blk = es.enter_context(nc.Block())
    ctx = dict(nc=nc, B=B, blk=blk, A=A, DR=DR, S=S, PS=PS, V=VIEWS,
               out_w=out_w, out_t=out_t, dbg=dbg, stage=stage, ffnsub=ffnsub)
    _emit_all(ctx)
    es.close()
    return nc



def _emit_all(ctx):
    _p_consts(ctx)
    _p_xload(ctx)
    wrest = None
    if ctx["stage"] >= 1:
        wunits = _p_weights_units(ctx)
        wunits[0]()
        wrest = wunits[1:]
    if ctx["stage"] >= 2:
        _p_ffn(ctx, 0, interleave=wrest)
    elif wrest:
        for f in wrest:
            f()
    if ctx["stage"] >= 3:
        _p_attn(ctx)
    if ctx["stage"] >= 4:
        _p_ffn(ctx, 1)
    _p_output(ctx)


def _p_consts(ctx):
    nc, B, blk, A, S, DR = (ctx[k] for k in ["nc", "B", "blk", "A", "S", "DR"])

    def gp_fn(gp):
        gp.iota(S["iotaE"][:, :], [[1, 64]], base=0, channel_multiplier=0,
                allow_small_or_imprecise_dtypes=True)
        gp.memset(S["ones"][:, :], 1.0)
        gp.memset(S["vt"][:, :, :], 1.0)
        gp.memset(S["xs"][:, :], 0.0)
        gp.memset(S["eps1"][:, :], 1e-5)
        for kt, t in enumerate(["t32a", "t32b", "t32c", "t32d"]):
            gp.iota(S[t][:, :], [[1, M]], base=-128 * kt, channel_multiplier=-1,
                    allow_small_or_imprecise_dtypes=True)
        gp.iota(S["trashp"][:, :], [[1, 1]], base=WIN, channel_multiplier=1,
                allow_small_or_imprecise_dtypes=True)
        B.i(gp.iota(S["f32x"][:, 0:128], [[1, 128]], base=0, channel_multiplier=-1,
                    allow_small_or_imprecise_dtypes=True), "gp")
    blk.gpsimd(gp_fn)

    def sy_fn(sy):
        B.i(sy.dma_start(out=S["f32x"][:, 512:1024].rearrange("p (a e) -> p a e", e=NE),
                         in_=A["gateT"]), "ld", 16)
        B.i(sy.dma_start(out=S["binS"][:, :], in_=A["b_in_c"]), "ld", 16)
        B.i(sy.dma_start(out=S["boutS"][:, :], in_=A["b_out_p"]), "ld", 16)
        B.i(sy.dma_start(out=S["maskp"][:, :], in_=A["maskpre"]), "ld", 16)
        B.i(sy.dma_start(out=S["tokb"][:, :], in_=A["tokbase"]), "ld", 16)
        B.i(sy.dma_start(out=S["slrows"][:, :], in_=A["slrow"]), "ld", 16)
    blk.sync(sy_fn)

    def ve_fn(ve):
        B.w(ve, "gp", "ld")
        ve.tensor_scalar(S["identF"][:, :], S["f32x"][:, 0:128], 0.0, None, ALU.is_equal)
        for t in ["t32a", "t32b", "t32c", "t32d"]:
            ve.tensor_scalar(S[t][:, :], S[t][:, :], 0.0, None, ALU.is_ge)
        ve.tensor_copy(S["identB"][:, :], S["identF"][:, :])
        for kt, t in enumerate(["t32a", "t32b", "t32c", "t32d"]):
            ve.tensor_copy(S["cmask"][:, kt, :], S[t][:, :])
        B.i(ve.tensor_copy(S["gateTs"][:, :, :],
                           S["f32x"][:, 512:1024].rearrange("p (a e) -> p a e", e=NE)), "ve")
    blk.vector(ve_fn)

    def gq_fn(gp):
        B.w(gp, "gp")
        B.i(gp.dma_start(out=DR["lnw_in"].ap()[SEG:SEG + EO_PAD, :],
                         in_=S["xs"][0:EO_PAD, 0:D]), "gq", 16)
        B.i(gp.dma_start(out=DR["eo_in"].ap()[NSLOT:EO_ROWS, :],
                         in_=S["xs"][0:EO_PAD, 0:D]), "gq", 16)
    blk.gpsimd(gq_fn)


def _p_xload(ctx):
    nc, B, blk, A, S, DR = (ctx[k] for k in ["nc", "B", "blk", "A", "S", "DR"])
    xin = [A["xw"], A["xt"]]
    for s in range(2):
        for ch in range(16):
            first = (s == 0 and ch == 0)

            def sy(sy, s=s, ch=ch, first=first):
                B.w(sy, "ve")
                B.i(sy.dma_start(out=S["f32x"][:, :],
                                 in_=xin[s][128 * ch:128 * (ch + 1), :]), "ld", 16)
            blk.sync(sy)

            def ve(ve, first=first):
                B.w(ve, "ld")
                if not first:
                    B.w(ve, "sq")
                src = S["f32x"][:, :].rearrange("p (g q f) -> p g q f", g=8, q=32)
                dst = S["tb16"][:, :].rearrange("p (g k q) -> p g k q", g=8, k=4)
                for k in range(4):
                    ins = ve.tensor_copy(dst[:, :, k, :], src[:, :, :, k])
                B.i(ins, "ve")
            blk.vector(ve)

            def sq(sc, s=s, ch=ch):
                B.w(sc, "ve")
                B.i(sc.dma_start(
                    out=DR["bounce"].ap()[SEG * s + 128 * ch:SEG * s + 128 * (ch + 1), :],
                    in_=S["tb16"][:, 0:D]), "sq", 16)
            blk.scalar(sq)

    def sy2(sy):
        B.w(sy, "sq")
        for t in range(8):
            B.i(sy.dma_start_transpose(S[f"res{t}"][:, :],
                                       DR["bounce"].ap()[:, 128 * t:128 * (t + 1)]), "ld", 16)
    blk.sync(sy2)


def _dequant_rowtile(ctx, pk_src, sc_src, ngrp):
    nc, B, blk, S = (ctx[k] for k in ["nc", "B", "blk", "S"])
    ncol = 32 * ngrp

    def syl(sy):
        B.w(sy, "sc")
        B.i(sy.dma_start(out=S["pk"][:, 0:ncol], in_=pk_src), "ld", 16)
        B.i(sy.dma_start(out=S["pksc"][:, 0:ngrp], in_=sc_src), "ld", 16)
    blk.sync(syl)

    def vef(ve):
        B.w(ve, "ld", "sc")
        ve.tensor_scalar(S["pksc"][:, 0:ngrp], S["pksc"][:, 0:ngrp],
                         float(2 ** 14), None, ALU.mult)
        for k in range(4):
            mag_sh = 9 - 4 * k
            ve.tensor_scalar(S["bita"][:, 0:ncol], S["pk"][:, 0:ncol], 7 << (4 * k),
                             abs(mag_sh), ALU.bitwise_and,
                             ALU.logical_shift_left if mag_sh >= 0 else ALU.logical_shift_right)
            ve.tensor_scalar(S["bitb"][:, 0:ncol], S["pk"][:, 0:ncol], 8 << (4 * k),
                             12 - 4 * k, ALU.bitwise_and, ALU.logical_shift_left)
            dst = S["bits"][:, 0:128 * ngrp].rearrange("p (g k q) -> p g k q", k=4, q=32)
            ins = ve.tensor_tensor(dst[:, :, k, :],
                                   S["bita"][:, 0:ncol].rearrange("p (g q) -> p g q", q=32),
                                   S["bitb"][:, 0:ncol].rearrange("p (g q) -> p g q", q=32),
                                   ALU.bitwise_or)
        B.i(ins, "ve")
    blk.vector(vef)

    def scf(ve):
        B.w(ve, "mm")
        for g in range(ngrp):
            ins = ve.tensor_scalar(S["wnat"][:, 128 * g:128 * (g + 1)],
                                   S["bits"][:, 128 * g:128 * (g + 1)].bitcast(F16),
                                   S["pksc"][:, g:g + 1], None, ALU.mult)
        B.i(ins, "sc")
    blk.vector(scf)


def _p_weights_units(ctx):
    nc, B, blk, A, S, DR, PS = (ctx[k] for k in ["nc", "B", "blk", "A", "S", "DR", "PS"])
    units = []

    def emit_set(sname, e):
        if sname == "sh":
            pk_gu, sc_gu, pk_dn, sc_dn = (A["sh_gup_pk"], A["sh_gup_sc"],
                                          A["sh_dn_pk"], A["sh_dn_sc"])
        else:
            pk_gu, sc_gu = A["gup_pk"][e], A["gup_sc"][e]
            pk_dn, sc_dn = A["dn_pk"][e], A["dn_sc"][e]

        for kind, rt in [("gu", rt) for rt in range(12)] + [("dn", rt) for rt in range(8)]:
            ngrp = 8 if kind == "gu" else 6
            pk_src = (pk_gu if kind == "gu" else pk_dn)[128 * rt:128 * (rt + 1), :]
            sc_src = (sc_gu if kind == "gu" else sc_dn)[128 * rt:128 * (rt + 1), :]
            _dequant_rowtile(ctx, pk_src, sc_src, ngrp)

            def mmf(mm, ngrp=ngrp):
                B.w(mm, "sc", "ve")
                for cc in range(ngrp):
                    ins = nc.tensor.transpose(PS["ptrA"][:, 128 * cc:128 * (cc + 1)],
                                              S["wnat"][:, 128 * cc:128 * (cc + 1)],
                                              S["identB"][:, :])
                B.i(ins, "mm")
            blk.tensor(mmf)

            if sname == "sh":
                def vee(ve, kind=kind, rt=rt, ngrp=ngrp):
                    B.w(ve, "mm", "sq")
                    if kind == "gu":
                        dst = S["wbuf"][:, 0:8 * GUW].rearrange("p (a b) -> p a b", a=8)
                        dst = dst[:, :, 128 * rt:128 * (rt + 1)]
                        src = PS["ptrA"][:, :].rearrange("p (a q) -> p a q", a=8)
                    else:
                        dst = S["wbuf"][:, 8 * GUW:].rearrange("p (a b) -> p a b", a=6)
                        dst = dst[:, :, 128 * rt:128 * (rt + 1)]
                        src = PS["ptrA"][:, 0:768].rearrange("p (a q) -> p a q", a=6)
                    B.i(ve.tensor_copy(dst, src), "ve")
                blk.vector(vee)
            else:
                def vee(ve, ngrp=ngrp):
                    B.w(ve, "mm", "sq")
                    B.i(ve.tensor_copy(S["tb16"][:, 0:128 * ngrp],
                                       PS["ptrA"][:, 0:128 * ngrp]), "ve")
                blk.vector(vee)

                def sqs(sc, kind=kind, rt=rt, ngrp=ngrp, e=e):
                    B.w(sc, "ve")
                    wst = DR["wstage"].ap()[e]
                    if kind == "gu":
                        dst = wst[:, 0:8 * GUW].rearrange("p (a b) -> p a b", a=8)
                    else:
                        dst = wst[:, 8 * GUW:].rearrange("p (a b) -> p a b", a=6)
                    dst = dst[:, :, 128 * rt:128 * (rt + 1)]
                    B.i(sc.dma_start(out=dst,
                                     in_=S["tb16"][:, 0:128 * ngrp].rearrange(
                                         "p (a b) -> p a b", b=128)), "sq", 16)
                blk.scalar(sqs)

        if sname == "sh":
            def sys(sc):
                B.w(sc, "ve")
                B.i(sc.dma_start(out=DR["shstage"].ap(), in_=S["wbuf"][:, :]), "sq", 16)
            blk.scalar(sys)

    units.append(lambda: emit_set("sh", None))
    for e in range(EPC):
        units.append(lambda e=e: emit_set("ex", e))

    def _cast_store(pairs):
        for sap, dap in pairs:
            def syj(sy, sap=sap):
                B.w(sy, "ve")
                B.i(sy.dma_start(out=S["f32x"][:, :], in_=sap), "ld", 16)
            blk.sync(syj)

            def vej(ve):
                B.w(ve, "ld", "sq")
                B.i(ve.tensor_copy(S["tb16"][:, 0:1024], S["f32x"][:, :]), "ve")
            blk.vector(vej)

            def sqj(sc, dap=dap):
                B.w(sc, "ve")
                B.i(sc.dma_start(out=dap, in_=S["tb16"][:, 0:1024]), "sq", 16)
            blk.scalar(sqj)

    pairs = []
    for a in range(8):
        for u in range(2):
            pairs.append((A["wqkvT"][128 * a:128 * (a + 1), 1024 * u:1024 * (u + 1)],
                          DR["attnw"].ap()[:, 2048 * a + 1024 * u:2048 * a + 1024 * (u + 1)]))
    for a in range(8):
        pairs.append((A["wqkvT"][128 * a:128 * (a + 1), 2048:3072],
                      DR["attnw"].ap()[:, 16384 + 2048 * a:16384 + 2048 * a + 1024]))
    for a in range(8):
        pairs.append((A["woutT"][128 * a:128 * (a + 1), :],
                      DR["attnw"].ap()[:, 16384 + 2048 * a + 1024:16384 + 2048 * (a + 1)]))
    units.append(lambda: _cast_store(pairs[0:16]))
    units.append(lambda: _cast_store(pairs[16:32]))
    return units


def _ln_chunk(ctx, ch, src, dstpref="lnc", width=512):
    nc, B, blk, S, PS = (ctx[k] for k in ["nc", "B", "blk", "S", "PS"])
    cols = slice(512 * ch, 512 * ch + width)

    def mm1(mm):
        B.w(mm, "ve", "sc")
        for t in range(8):
            ins = nc.tensor.matmul(PS["psD"][:, 0:width], S["ones"][:, :],
                                   S[src + str(t)][:, cols], start=(t == 0), stop=(t == 7))
        B.i(ins, "mm")
    blk.tensor(mm1)

    for t in range(8):
        def scx(ve, t=t):
            B.w(ve, "mm")
            B.i(ve.tensor_tensor(S["tb16b"][:, 0:width], S[src + str(t)][:, cols],
                                 S[src + str(t)][:, cols], ALU.mult), "sc")
        blk.vector(scx)

        def mm2(mm, t=t):
            B.w(mm, "sc")
            B.i(nc.tensor.matmul(PS["psE"][:, 0:width], S["ones"][:, :],
                                 S["tb16b"][:, 0:width], start=(t == 0), stop=(t == 7)), "mm")
        blk.tensor(mm2)

    def ve1(ve):
        B.w(ve, "mm")
        ve.tensor_scalar(S["t32a"][:, 0:width], PS["psD"][:, 0:width], 1.0 / D, None, ALU.mult)
        ve.tensor_scalar(S["t32b"][:, 0:width], PS["psE"][:, 0:width], 1.0 / D, None, ALU.mult)
        ve.tensor_tensor(S["t32c"][:, 0:width], S["t32a"][:, 0:width], S["t32a"][:, 0:width],
                         ALU.mult)
        ve.tensor_tensor(S["t32b"][:, 0:width], S["t32b"][:, 0:width],
                         S["t32c"][:, 0:width], ALU.subtract)
        ve.tensor_scalar(S["t32b"][:, 0:width], S["t32b"][:, 0:width], 1e-5,
                         None, ALU.add)
        B.i(ve.reciprocal(S["t32c"][:, 0:width], S["t32b"][:, 0:width]), "ve")
    blk.vector(ve1)

    def sc1(sc):
        B.w(sc, "ve")
        B.i(sc.activation(out=S["t32d"][:, 0:width], in_=S["t32c"][:, 0:width],
                          func=AF.Sqrt), "sc")
    blk.scalar(sc1)

    def ve2(ve):
        B.w(ve, "sc", "mm", "sq")
        for t in range(8):
            ve.tensor_tensor(S["tb16b"][:, 0:width], S[src + str(t)][:, cols],
                             S["t32a"][:, 0:width], ALU.subtract)
            ins = ve.tensor_tensor(S[dstpref + str(t)][:, 0:width], S["tb16b"][:, 0:width],
                                   S["t32d"][:, 0:width], ALU.mult)
        B.i(ins, "ve")
    blk.vector(ve2)


def _fmv(ctx):
    return (ctx["DR"]["bounce"].ap().rearrange("tk d -> (tk d)")[0:D * SEG]
            .rearrange("(d s) -> d s", d=D))


def _stage_and_route(ctx, call, ch):
    nc, B, blk, A, S, DR, PS = (ctx[k] for k in ["nc", "B", "blk", "A", "S", "DR", "PS"])
    fmv = _fmv(ctx)

    def sq2(sc):
        B.w(sc, "ve", "ld")
        for t in range(8):
            B.i(sc.dma_start(out=fmv[128 * t:128 * (t + 1), 512 * ch:512 * (ch + 1)],
                             in_=S[f"lnc{t}"][:, :]), "sq", 16)
    blk.scalar(sq2)

    for sub in range(4):
        wc = 4 * ch + sub
        tcols = slice(128 * sub, 128 * (sub + 1))

        def mmr(mm, tcols=tcols):
            B.w(mm, "ve", "sc")
            for t in range(8):
                ins = nc.tensor.matmul(PS["psF"][:, 0:NE], S[f"lnc{t}"][:, tcols],
                                       S["gateTs"][:, t, :], start=(t == 0), stop=(t == 7))
            B.i(ins, "mm")
        blk.tensor(mmr)

        def ver(ve, wc=wc):
            B.w(ve, "mm")
            ve.tensor_copy(S["logit"][:, wc, :], PS["psF"][:, 0:NE])
            ve.max(S["mx8"][:, wc, :], S["logit"][:, wc, :])
            B.i(ve.tensor_scalar(S["t32a"][:, 0:1], S["mx8"][:, wc, 0:1], -1.0, None,
                                 ALU.mult), "ve")
        blk.vector(ver)

        def scr(sc, wc=wc):
            B.w(sc, "ve")
            B.i(sc.activation(out=S["wj4"][:, wc, :], in_=S["mx8"][:, wc, 0:4], func=AF.Exp,
                              bias=S["t32a"][:, 0:1], scale=1.0), "sc")
        blk.scalar(scr)

        def ver2(ve, wc=wc):
            B.w(ve, "sc")
            ve.reduce_sum(S["t32b"][:, 0:1], S["wj4"][:, wc, :], axis=AX.X)
            ve.tensor_scalar(S["amask"][:, :], S["logit"][:, wc, :], S["mx8"][:, wc, 3:4],
                             None, ALU.is_ge)
            ve.reciprocal(S["t32c"][:, 0:1], S["t32b"][:, 0:1])
            ve.tensor_copy(S["t32a"][:, 0:64], S["logit"][:, wc, :])
            B.i(ve.tensor_scalar(S["wj4"][:, wc, :], S["wj4"][:, wc, :], S["t32c"][:, 0:1],
                                 None, ALU.mult), "ve")
        blk.vector(ver2)

        def mmt(mm):
            B.w(mm, "ve")
            B.i(nc.tensor.transpose(PS["ptrB"][0:64, 0:128], S["amask"][:, :],
                                    S["identB"][:, :]), "mm")
        blk.tensor(mmt)

        def vet(ve, wc=wc):
            B.w(ve, "mm")
            B.i(ve.tensor_copy(S["scanb"][0:64, 128 * wc:128 * (wc + 1)],
                               PS["ptrB"][0:64, 0:128]), "ve")
        blk.vector(vet)


def _shared_chunk(ctx, ch):
    nc, B, blk, S, PS = (ctx[k] for k in ["nc", "B", "blk", "S", "PS"])
    cols = slice(512 * ch, 512 * (ch + 1))

    for ip in range(6):
        def mmg(mm, ip=ip):
            B.w(mm, "ve", "sc")
            for t in range(8):
                nc.tensor.matmul(PS["psA"][:, :], S["wbuf"][:, GUW * t + 128 * ip:GUW * t + 128 * (ip + 1)],
                                 S[f"lnc{t}"][:, :], start=(t == 0), stop=(t == 7))
            for t in range(8):
                ins = nc.tensor.matmul(PS["psB"][:, :],
                                       S["wbuf"][:, GUW * t + 128 * (ip + 6):GUW * t + 128 * (ip + 7)],
                                       S[f"lnc{t}"][:, :], start=(t == 0), stop=(t == 7))
            B.i(ins, "mm")
        blk.tensor(mmg)

        def scg(sc):
            B.w(sc, "mm")
            B.i(sc.activation(out=S["tb16b"][:, 0:512], in_=PS["psA"][:, :], func=AF.Silu), "sc")
        blk.scalar(scg)

        def veg(ve, ip=ip):
            B.w(ve, "sc")
            B.i(ve.tensor_tensor(S["h6"][:, ip, :], S["tb16b"][:, 0:512], PS["psB"][:, :],
                                 ALU.mult), "ve")
        blk.vector(veg)

    for dt in range(8):
        def mmd(mm, dt=dt):
            B.w(mm, "ve", "sc")
            for ip in range(6):
                ins = nc.tensor.matmul(PS["psC"][:, :],
                                       S["wbuf"][:, 8 * GUW + D * ip + 128 * dt:8 * GUW + D * ip + 128 * (dt + 1)],
                                       S["h6"][:, ip, :], start=(ip == 0), stop=(ip == 5))
            B.i(ins, "mm")
        blk.tensor(mmd)

        def ved(ve, dt=dt):
            B.w(ve, "mm")
            B.i(ve.scalar_tensor_tensor(S[f"res{dt}"][:, cols], PS["psC"][:, :], 0.5,
                                        S[f"res{dt}"][:, cols], ALU.mult, ALU.add), "ve")
        blk.vector(ved)


def _stage_transpose_ag(ctx, call):
    nc, B, blk, S, DR = (ctx[k] for k in ["nc", "B", "blk", "S", "DR"])
    fmv = _fmv(ctx)
    for w in range(16):
        def sy1(sy, w=w):
            B.w(sy, "sq", "gq")
            B.i(sy.dma_start_transpose(S["eo"][:, :], fmv[:, 128 * w:128 * (w + 1)]), "ld", 16)
        blk.sync(sy1)

        def gq1(gp, w=w):
            B.w(gp, "ld", "cc")
            B.i(gp.dma_start(out=DR["lnw_in"].ap()[128 * w:128 * (w + 1), :],
                             in_=S["eo"][:, :]), "gq", 16)
        blk.gpsimd(gq1)

    def gcc(gp):
        B.w(gp, "gq")
        B.i(gp.collective_compute("AllGather", ALU.bypass, replica_groups=[list(range(NCORE))],
                                  ins=[DR["lnw_in"].ap().opt()],
                                  outs=[ctx["V"]["lnw_out"].opt()]), "cc")
    blk.gpsimd(gcc)


def _scan_phase(ctx, call):
    nc, B, blk, S, DR, PS = (ctx[k] for k in ["nc", "B", "blk", "S", "DR", "PS"])

    def ve1(ve):
        B.i(ve.reduce_sum(S["cnt8"][0:64, 0:1], S["scanb"][0:64, :], axis=AX.X), "ve")
    blk.vector(ve1)

    def gq1(gp):
        B.w(gp, "ve")
        B.i(gp.dma_start(out=DR["cnt_in"].ap(), in_=S["cnt8"][0:64, 0:1]), "gq", 16)
        B.w(gp, "gq")
        B.i(gp.collective_compute("AllGather", ALU.bypass, replica_groups=[list(range(NCORE))],
                                  ins=[DR["cnt_in"].ap().opt()],
                                  outs=[ctx["V"]["cnt_out"].opt()]), "cc")
        B.w(gp, "cc")
        B.i(gp.dma_start(out=S["t32a"][0:8, 0:64],
                         in_=ctx["V"]["cnt_out"].rearrange("(k e) c -> k (e c)", e=64)), "gq", 16)
    blk.gpsimd(gq1)

    def mmc(mm):
        B.w(mm, "gq", "ve")
        B.i(nc.tensor.transpose(PS["psF"][0:64, 0:8], S["t32a"][0:8, 0:64],
                                S["identF"][0:8, 0:8]), "mm")
    blk.tensor(mmc)

    def vec8(ve):
        B.w(ve, "mm")
        B.i(ve.tensor_copy(S["cnt8"][0:64, 0:8], PS["psF"][0:64, 0:8]), "ve")
    blk.vector(vec8)

    def ve2(ve):
        B.w(ve, "gq")
        ve.memset(S["base"][0:64, :], 0.0)
        for k in range(NCORE):
            ve.scalar_tensor_tensor(S["base"][0:64, :], S["cnt8"][0:64, k:k + 1],
                                    S["maskp"][0:64, k:k + 1], S["base"][0:64, :],
                                    ALU.mult, ALU.add)
        ve.tensor_tensor_scan(S["scanb"][0:64, :], S["scanb"][0:64, :], S["scanb"][0:64, :],
                              0.0, ALU.add, ALU.bypass)
        B.i(ve.tensor_scalar(S["scanb"][0:64, :], S["scanb"][0:64, :], S["base"][0:64, 0:1],
                             None, ALU.add), "ve")
    blk.vector(ve2)


def _sigma_scatter(ctx, call):
    nc, B, blk, A, S, DR, PS = (ctx[k] for k in ["nc", "B", "blk", "A", "S", "DR", "PS"])

    for wc in range(16):
        def mm1(mm, wc=wc):
            B.w(mm, "ve")
            B.i(nc.tensor.transpose(PS["psF"][:, 0:64],
                                    S["scanb"][0:64, 128 * wc:128 * (wc + 1)],
                                    S["identF"][0:64, 0:64]), "mm")
        blk.tensor(mm1)

        def vsig(ve, wc=wc):
            B.w(ve, "mm", "gq")
            ve.tensor_copy(S["post"][:, :], PS["psF"][:, 0:64])
            for j in range(4):
                ve.tensor_scalar(S["t32a"][:, 0:64], S["logit"][:, wc, :],
                                 S["mx8"][:, wc, j:j + 1], None, ALU.is_equal)
                ve.tensor_tensor(S["t32b"][:, 0:64], S["post"][:, :], S["t32a"][:, 0:64],
                                 ALU.mult)
                ve.reduce_sum(S["t32c"][:, j:j + 1], S["t32b"][:, 0:64], axis=AX.X)
                ve.tensor_tensor(S["t32b"][:, 0:64], S["iotaE"][:, :], S["t32a"][:, 0:64],
                                 ALU.mult)
                ve.reduce_sum(S["t32d"][:, j:j + 1], S["t32b"][:, 0:64], axis=AX.X)
            ia, ib, ic = S["i32a"], S["i32b"], S["i32c"]
            ve.tensor_scalar(S["t32c"][:, 0:4], S["t32c"][:, 0:4], -1.0, None, ALU.add)
            ve.tensor_scalar(S["scat4"][:, 0, 0:1], S["tokb"][:, :], float(128 * wc),
                             None, ALU.add)
            ve.tensor_copy(ia[:, 0:4], S["t32d"][:, 0:4])
            ve.tensor_scalar(S["t32a"][:, 0:4], S["t32c"][:, 0:4], float(CAP), None,
                             ALU.is_ge)
            ve.tensor_copy(S["scat4"][:, 0, 1:2], S["wj4"][:, wc, 0:1])
            ve.tensor_copy(ib[:, 0:4], S["t32c"][:, 0:4])
            ve.tensor_scalar(ia[:, 4:8], ia[:, 0:4], 8, None,
                             ALU.logical_shift_left)
            ve.tensor_scalar(S["t32b"][:, 0:4], S["t32a"][:, 0:4], -1.0, 1.0,
                             ALU.mult, ALU.add)
            ve.tensor_scalar(ia[:, 8:12], ia[:, 0:4], 3, 4,
                             ALU.logical_shift_right, ALU.logical_shift_left)
            ve.tensor_copy(ic[:, 0:4], S["t32a"][:, 0:4])
            ve.tensor_tensor(ib[:, 4:8], ia[:, 4:8], ib[:, 0:4], ALU.add)
            ve.tensor_copy(ic[:, 4:8], S["t32b"][:, 0:4])
            ve.tensor_scalar(S["scat4"][:, 1, 0:1], S["tokb"][:, :], float(128 * wc),
                             None, ALU.add)
            ve.tensor_scalar(ic[:, 8:12], ic[:, 0:4], 11, None,
                             ALU.logical_shift_left)
            ve.tensor_scalar(ia[:, 12:16], ib[:, 4:8], 127, 7,
                             ALU.bitwise_and, ALU.logical_shift_left)
            ve.tensor_tensor(ia[:, 16:20], ia[:, 8:12], ib[:, 4:8], ALU.add)
            ve.tensor_scalar(ia[:, 20:24], ib[:, 4:8], 7, None,
                             ALU.logical_shift_right)
            ve.tensor_scalar(S["t32c"][:, 4:8], S["t32a"][:, 0:4], S["trashp"][:, 0:1],
                             None, ALU.mult)
            ve.tensor_tensor(ia[:, 24:28], ia[:, 16:20], ic[:, 4:8], ALU.mult)
            ve.tensor_tensor(ia[:, 28:32], ia[:, 12:16], ia[:, 20:24], ALU.add)
            ve.tensor_copy(S["scat4"][:, 1, 1:2], S["wj4"][:, wc, 1:2])
            ve.tensor_copy(ic[:, 12:16], S["t32c"][:, 4:8])
            ve.tensor_scalar(S["scat4"][:, 2, 0:1], S["tokb"][:, :], float(128 * wc),
                             None, ALU.add)
            ve.tensor_tensor(S["sig"][:, wc, :], ia[:, 24:28], ic[:, 8:12], ALU.add)
            ve.tensor_tensor(ia[:, 32:36], ia[:, 28:32], ic[:, 4:8], ALU.mult)
            ve.tensor_copy(S["scat4"][:, 2, 1:2], S["wj4"][:, wc, 2:3])
            ve.tensor_scalar(S["scat4"][:, 3, 0:1], S["tokb"][:, :], float(128 * wc),
                             None, ALU.add)
            ve.tensor_tensor(ib[:, 4:8], ia[:, 32:36], ic[:, 12:16], ALU.add)
            B.i(ve.tensor_copy(S["scat4"][:, 3, 1:2], S["wj4"][:, wc, 3:4]), "ve")
        blk.vector(vsig)

        for j in range(4):
            def gsc(gp, j=j):
                B.w(gp, "ve")
                B.i(gp.indirect_dma_start(DR["tbl_in"].ap(),
                                          IndirectOffsetOnAxis(S["i32b"][:, 4 + j:5 + j], 0),
                                          S["scat4"][:, j, :], None,
                                          compute_op=ALU.add), "gq", 16)
            blk.gpsimd(gsc)

    def gar(gp):
        B.w(gp, "gq")
        B.i(gp.collective_compute("AllGather", ALU.bypass, replica_groups=[list(range(NCORE))],
                                  ins=[DR["tbl_in"].ap().opt()],
                                  outs=[ctx["V"]["tbl_ag"].opt()]), "cc")
    blk.gpsimd(gar)

    def syt0(sy):
        B.w(sy, "cc")
        B.i(sy.dma_start(out=S["tbb"][:, 0, :], in_=ctx["V"]["tbl_rk"](0)), "ld", 16)
        B.i(sy.dma_start(out=S["tbb"][:, 1, :], in_=ctx["V"]["tbl_rk"](1)), "ld", 16)
    blk.sync(syt0)
    for r in range(2, NCORE + 1):
        def vetk(ve):
            B.w(ve, "ld")
            B.i(ve.tensor_tensor(S["tbb"][:, 0, :], S["tbb"][:, 0, :], S["tbb"][:, 1, :],
                                 ALU.add), "ve")
        blk.vector(vetk)
        if r < NCORE:
            def sytk(sy, r=r):
                B.w(sy, "ve")
                B.i(sy.dma_start(out=S["tbb"][:, 1, :], in_=ctx["V"]["tbl_rk"](r)), "ld", 16)
            blk.sync(sytk)

    def gsum(gp):
        B.w(gp, "ve")
        B.i(gp.dma_start(out=DR["tblsum"].ap().rearrange("(p a) c -> p (a c)", p=128),
                         in_=S["tbb"][:, 0, :]), "gq", 16)
        B.w(gp, "gq")
        B.i(gp.indirect_dma_start(S["tslice"][:, :], None, DR["tblsum"].ap(),
                                  IndirectOffsetOnAxis(S["slrows"][:, :], 0)), "gq", 16)
    blk.gpsimd(gsum)

    def vtw(ve):
        B.w(ve, "gq")
        tsv = S["tslice"][:, :].rearrange("p (n c) -> p n c", c=2)
        ve.tensor_copy(S["tidx"][:, :], tsv[:, :, 0])
        ve.tensor_scalar(S["twgt"][:, :], tsv[:, :, 1], 0.5, None, ALU.mult)
        ve.tensor_copy(S["t32a"][:, 0:16], tsv[:, :, 1])
        ve.tensor_scalar(S["i32a"][:, 0:16], S["tidx"][:, :], 11, 4,
                         ALU.logical_shift_right, ALU.logical_shift_left)
        ve.tensor_copy(S["t32b"][:, 0:16], S["t32a"][:, 0:16])
        ve.tensor_copy(S["t32c"][:, 0:16], S["t32a"][:, 0:16])
        B.i(ve.tensor_tensor(S["tidx"][:, :], S["tidx"][:, :], S["i32a"][:, 0:16],
                             ALU.add), "ve")
    blk.vector(vtw)


def _dbg_dump(ctx):
    nc, B, blk, S, DR, PS = (ctx[k] for k in ["nc", "B", "blk", "S", "DR", "PS"])
    dbg = ctx["dbg"]

    def gd(gp):
        B.w(gp, "ve", "gq", "cc", "mm", "sc")
        B.i(gp.dma_start(out=dbg[:, 0:32], in_=S["tslice"][:, :]), "gq", 16)
        B.i(gp.dma_start(out=dbg[:, 32:96].bitcast(I32).rearrange("p (a b) -> p a b", b=4),
                         in_=S["sig"][:, :, :]), "gq", 16)
        B.i(gp.dma_start(out=dbg[0:64, 96:104], in_=S["cnt8"][0:64, :]), "gq", 16)
        B.i(gp.dma_start(out=dbg[0:64, 112:120], in_=S["cnt8"][0:64, :]), "gq", 16)
        B.i(gp.dma_start(out=dbg[0:8, 120:184], in_=S["t32a"][0:8, 0:64]), "gq", 16)
        B.i(gp.dma_start(out=dbg[0:64, 184:192], in_=ctx["V"]["cnt_out"].rearrange("(k e) c -> e k c", e=64)[0:64, 0:8, 0], allow_slow_non_contiguous=True), "gq", 16)
        B.i(gp.dma_start(out=dbg[:, 128:192], in_=S["logit"][:, 0, :]), "gq", 16)
        B.i(gp.dma_start(out=dbg[:, 192:200], in_=S["mx8"][:, 0, :]), "gq", 16)
        B.i(gp.dma_start(out=dbg[0:64, 1024:3072], in_=S["scanb"][0:64, :]), "gq", 16)
        B.i(gp.dma_start(out=dbg[:, 3072:3584], in_=S["lnc0"][:, :]), "gq", 16)
        B.i(gp.dma_start(out=dbg[:, 232:1000].bitcast(BF16)[:, 0:1536], in_=S["wbuf"][:, 0:1536]), "gq", 16)
        B.i(gp.dma_start(out=dbg[:, 3584:4096].bitcast(BF16)[:, 0:1024],
                         in_=S["wbuf"][:, 8 * GUW:8 * GUW + 1024]), "gq", 16)
        B.i(gp.dma_start(out=dbg[:, 200:216], in_=S["twgt"][:, :]), "gq", 16)
        B.i(gp.dma_start(out=dbg[:, 216:232].bitcast(I32), in_=S["tidx"][:, :]), "gq", 16)
        B.w(gp, "gq")
    blk.gpsimd(gd)


def _experts(ctx, call):
    nc, B, blk, S, DR, PS = (ctx[k] for k in ["nc", "B", "blk", "S", "DR", "PS"])

    for e in range(EPC):
        def syw(sy, e=e):
            B.w(sy, "mm", "sq")
            B.i(sy.dma_start(out=S["wbuf"][:, :], in_=DR["wstage"].ap()[e]), "ld", 16)
        blk.sync(syw)

        for s in range(2):
            def gg(gp, e=e, s=s):
                B.w(gp, "ve")
                B.w(gp, "mm")
                B.i(gp.indirect_dma_start(S["gb"][:, s, :], None, ctx["V"]["lnw_out"],
                                          IndirectOffsetOnAxis(S["tidx"][:, 2 * e + s:2 * e + s + 1], 0)),
                    "gq", 16)
            blk.gpsimd(gg)

        for s in range(2):
            def mmt(mm, s=s):
                B.w(mm, "gq", "ve")
                for cc in range(8):
                    ins = nc.tensor.transpose(PS["ptrB"][:, 128 * cc:128 * (cc + 1)],
                                              S["gb"][:, s, 128 * cc:128 * (cc + 1)],
                                              S["identB"][:, :])
                B.i(ins, "mm")
            blk.tensor(mmt)

            def vet(ve, s=s):
                B.w(ve, "mm")
                B.i(ve.tensor_copy(S["gbT"][:, :, 128 * s:128 * (s + 1)],
                                   PS["ptrB"][:, :].rearrange("p (a q) -> p a q", a=8)), "ve")
            blk.vector(vet)

        for ip in range(6):
            def mmg(mm, ip=ip):
                B.w(mm, "ve", "sc", "ld")
                for t in range(8):
                    nc.tensor.matmul(PS["psA"][:, 0:256],
                                     S["wbuf"][:, GUW * t + 128 * ip:GUW * t + 128 * (ip + 1)],
                                     S["gbT"][:, t, :], start=(t == 0), stop=(t == 7))
                for t in range(8):
                    ins = nc.tensor.matmul(PS["psB"][:, 0:256],
                                           S["wbuf"][:, GUW * t + 128 * (ip + 6):GUW * t + 128 * (ip + 7)],
                                           S["gbT"][:, t, :], start=(t == 0), stop=(t == 7))
                B.i(ins, "mm")
            blk.tensor(mmg)

            def scg(sc):
                B.w(sc, "mm")
                B.i(sc.activation(out=S["tb16b"][:, 0:256], in_=PS["psA"][:, 0:256],
                                  func=AF.Silu), "sc")
            blk.scalar(scg)

            def veg(ve, ip=ip):
                B.w(ve, "sc")
                B.i(ve.tensor_tensor(S["h6"][:, ip, 0:256], S["tb16b"][:, 0:256],
                                     PS["psB"][:, 0:256], ALU.mult), "ve")
            blk.vector(veg)

        for s in range(2):
            for dh in range(2):
                def mmd(mm, s=s, dh=dh):
                    B.w(mm, "ve", "sc")
                    for ip in range(6):
                        ins = nc.tensor.matmul(
                            PS["psC"][:, :],
                            S["h6"][:, ip, 128 * s:128 * (s + 1)],
                            S["wbuf"][:, 8 * GUW + D * ip + 512 * dh:8 * GUW + D * ip + 512 * (dh + 1)],
                            start=(ip == 0), stop=(ip == 5))
                    B.i(ins, "mm")
                blk.tensor(mmd)

                def ved(ve, e=e, s=s, dh=dh):
                    B.w(ve, "mm", "gq")
                    B.i(ve.tensor_scalar(S["eo"][:, 512 * dh:512 * (dh + 1)], PS["psC"][:, :],
                                         S["twgt"][:, 2 * e + s:2 * e + s + 1], None,
                                         ALU.mult), "ve")
                blk.vector(ved)

            def geo(gp, e=e, s=s):
                B.w(gp, "ve", "cc")
                B.i(gp.dma_start(
                    out=DR["eo_in"].ap()[0:NSLOT, :].rearrange("(n p) d -> p n d", p=128)[:, 2 * e + s],
                    in_=S["eo"][:, :]), "gq", 16)
            blk.gpsimd(geo)


def _combine(ctx, call):
    nc, B, blk, S, DR, PS = (ctx[k] for k in ["nc", "B", "blk", "S", "DR", "PS"])

    def gcc(gp):
        B.w(gp, "gq")
        B.i(gp.collective_compute("AllGather", ALU.bypass, replica_groups=[list(range(NCORE))],
                                  ins=[DR["eo_in"].ap().opt()],
                                  outs=[ctx["V"]["eo_out"].opt()]), "cc")
        B.w(gp, "cc")
    blk.gpsimd(gcc)

    for wc in range(16):
        for half in range(2):
            for j2 in range(2):
                def gx(gp, wc=wc, half=half, j2=j2):
                    B.w(gp, "ve")
                    j = 2 * half + j2
                    B.i(gp.indirect_dma_start(S["gb"][:, j2, :], None, ctx["V"]["gall"],
                                              IndirectOffsetOnAxis(S["sig"][:, wc, j:j + 1], 0),
                                              element_offset=ctx["V"]["eo_row0"] * D),
                        "gq", 16)
                blk.gpsimd(gx)

            def vsum(ve, half=half):
                B.w(ve, "gq")
                if half == 0:
                    B.i(ve.tensor_tensor(S["xs"][:, :], S["gb"][:, 0, :], S["gb"][:, 1, :],
                                         ALU.add), "ve")
                else:
                    ve.tensor_tensor(S["tb16b"][:, 0:D], S["gb"][:, 0, :], S["gb"][:, 1, :],
                                     ALU.add)
                    B.i(ve.tensor_tensor(S["xs"][:, :], S["xs"][:, :], S["tb16b"][:, 0:D],
                                         ALU.add), "ve")
            blk.vector(vsum)

        def mmx(mm):
            B.w(mm, "ve")
            for cc in range(8):
                ins = nc.tensor.transpose(PS["ptrA"][:, 128 * cc:128 * (cc + 1)],
                                          S["xs"][:, 128 * cc:128 * (cc + 1)],
                                          S["identB"][:, :])
            B.i(ins, "mm")
        blk.tensor(mmx)

        def vadd(ve, wc=wc):
            B.w(ve, "mm")
            for t in range(8):
                ins = ve.scalar_tensor_tensor(S[f"res{t}"][:, 128 * wc:128 * (wc + 1)],
                                              PS["ptrA"][:, 128 * t:128 * (t + 1)], 1.0,
                                              S[f"res{t}"][:, 128 * wc:128 * (wc + 1)],
                                              ALU.mult, ALU.add)
            B.i(ins, "ve")
        blk.vector(vadd)


def _p_ffn(ctx, call, interleave=None):
    nc, B, blk, S, DR = (ctx[k] for k in ["nc", "B", "blk", "S", "DR"])
    sub = ctx.get("ffnsub", 99)
    units = []

    def u_pre():
        def gz(gp):
            B.w(gp, "cc")
            B.w(gp, "ve")
            B.i(gp.memset(S["t32d"][:, 0:2 * (TBL_ROWS // 128)], 0.0), "gp")
            tv = DR["tbl_in"].ap().rearrange("(p a) c -> p a c", p=128)
            B.i(gp.dma_start(out=tv,
                             in_=S["t32d"][:, 0:2 * (TBL_ROWS // 128)].rearrange(
                                 "p (a c) -> p a c", c=2)), "gq", 16)
        blk.gpsimd(gz)

        if call == 1:
            def syw(sy):
                B.w(sy, "mm", "sq")
                B.i(sy.dma_start(out=S["wbuf"][:, :], in_=DR["shstage"].ap()), "ld", 16)
            blk.sync(syw)
    units.append(u_pre)

    for ch in range(8):
        def u_ch(ch=ch):
            _ln_chunk(ctx, ch, "res")
            if ch < 4 and sub >= 1:
                _stage_and_route(ctx, call, ch)
            if sub >= 2:
                _shared_chunk(ctx, ch)
        units.append(u_ch)

    def u_sta():
        if sub >= 3:
            _stage_transpose_ag(ctx, call)
        if sub >= 4:
            _scan_phase(ctx, call)
    units.append(u_sta)

    def u_sig():
        if sub >= 5:
            _sigma_scatter(ctx, call)
            if ctx["dbg"] is not None and call == 0:
                _dbg_dump(ctx)
    units.append(u_sig)

    other = list(interleave) if interleave else []
    i = j = 0
    while i < len(units) or j < len(other):
        if i < len(units):
            units[i]()
            i += 1
        if j < len(other):
            other[j]()
            j += 1

    if sub >= 6:
        _experts(ctx, call)
    if sub >= 7:
        _combine(ctx, call)


def _p_attn(ctx):
    nc, B, blk, S, DR, PS = (ctx[k] for k in ["nc", "B", "blk", "S", "DR", "PS"])

    for seg in range(2):
        B0 = SEG * seg

        def sywA(sy):
            B.w(sy, "mm", "sq")
            B.i(sy.dma_start(out=S["wbuf"][:, 0:16384], in_=DR["attnw"].ap()[:, 0:16384]),
                "ld", 16)
        blk.sync(sywA)

        def vec(ve, B0=B0):
            B.w(ve, "ve")
            for t in range(8):
                src = S[f"res{t}"][:, :].rearrange("p (a s r) -> p a s r", a=2, r=4)[:, seg, :, 0]
                ins = ve.tensor_copy(S[f"lnc{t}"][:, :], src)
            B.i(ins, "ve")
        blk.vector(vec)
        _ln_chunk(ctx, 0, "lnc")

        for ec in range(16):
            def mmq(mm, ec=ec):
                B.w(mm, "ve", "sc", "ld")
                for t in range(8):
                    ins = nc.tensor.matmul(PS["psA"][:, :],
                                           S["wbuf"][:, 2048 * t + 128 * ec:2048 * t + 128 * (ec + 1)],
                                           S[f"lnc{t}"][:, :], start=(t == 0), stop=(t == 7))
                B.i(ins, "mm")
            blk.tensor(mmq)

            def scq(ve, ec=ec):
                B.w(ve, "mm")
                B.i(ve.tensor_scalar(S["qkv"][:, ec, :], PS["psA"][:, :],
                                     S["binS"][:, ec:ec + 1], None, ALU.add), "ve")
            blk.vector(scq)

        def sywB(sy):
            B.w(sy, "mm", "sc", "sq")
            B.i(sy.dma_start(out=S["wbuf"][:, 0:16384],
                             in_=DR["attnw"].ap()[:, 16384:2 * 16384]), "ld", 16)
        blk.sync(sywB)

        for ec in range(8):
            def mmv(mm, ec=ec):
                B.w(mm, "ve", "sc", "ld")
                for t in range(8):
                    ins = nc.tensor.matmul(PS["psA"][:, :],
                                           S["wbuf"][:, 2048 * t + 128 * ec:2048 * t + 128 * (ec + 1)],
                                           S[f"lnc{t}"][:, :], start=(t == 0), stop=(t == 7))
                B.i(ins, "mm")
            blk.tensor(mmv)

            def scv(ve, ec=ec):
                B.w(ve, "mm")
                B.i(ve.tensor_scalar(S["qkv"][:, 16 + ec, :], PS["psA"][:, :],
                                     S["binS"][:, 16 + ec:17 + ec], None, ALU.add), "ve")
            blk.vector(scv)

        for h in range(NH):
            r0 = 64 * (h % 2)
            qh = S["qkv"][r0:r0 + 64, h // 2, :]
            kh = S["qkv"][r0:r0 + 64, 8 + h // 2, :]
            vh = S["qkv"][r0:r0 + 64, 16 + h // 2, :]

            sbank = ["psC", "psD", "psE", "psF"]

            def mms(mm, kh=kh, qh=qh, vh=vh, r0=r0):
                B.w(mm, "sc", "ve")
                for kt in range(4):
                    nc.tensor.matmul(PS[sbank[kt]][:, 128 * kt:512],
                                     kh[:, 128 * kt:128 * (kt + 1)], qh[:, 128 * kt:512],
                                     start=True, stop=True)
                for kt in range(4):
                    ins = nc.tensor.transpose(PS["ptrB"][:, 64 * kt:64 * (kt + 1)],
                                              vh[:, 128 * kt:128 * (kt + 1)],
                                              S["identB"][r0:r0 + 64, r0:r0 + 64])
                B.i(ins, "mm")
            blk.tensor(mms)

            def scs(sc):
                B.w(sc, "mm")
                sc.activation(out=S["tb16b"][:, 0:512], in_=PS["psC"][:, 0:512],
                              func=AF.Exp, scale=0.125)
                sc.activation(out=S["tb16b"][:, 512:896], in_=PS["psD"][:, 128:512],
                              func=AF.Exp, scale=0.125)
                sc.activation(out=S["tb16"][:, 0:256], in_=PS["psE"][:, 256:512],
                              func=AF.Exp, scale=0.125)
                B.i(sc.activation(out=S["tb16"][:, 256:384], in_=PS["psF"][:, 384:512],
                                  func=AF.Exp, scale=0.125), "sc")
            blk.scalar(scs)

            def ves(ve):
                B.w(ve, "sc", "mm")
                ve.tensor_tensor(S["pt"][:, 0, :], S["tb16b"][:, 0:512],
                                 S["cmask"][:, 0, :], ALU.mult)
                ve.tensor_tensor(S["pt"][:, 1, 128:512], S["tb16b"][:, 512:896],
                                 S["cmask"][:, 1, 128:512], ALU.mult)
                ve.tensor_tensor(S["pt"][:, 2, 256:512], S["tb16"][:, 0:256],
                                 S["cmask"][:, 2, 256:512], ALU.mult)
                ve.tensor_tensor(S["pt"][:, 3, 384:512], S["tb16"][:, 256:384],
                                 S["cmask"][:, 3, 384:512], ALU.mult)
                for kt in range(4):
                    ins = ve.tensor_copy(S["vt"][:, kt, 0:64],
                                         PS["ptrB"][:, 64 * kt:64 * (kt + 1)])
                B.i(ins, "ve")
            blk.vector(ves)

            def mmpv(mm):
                B.w(mm, "ve", "sc")
                for qt in range(4):
                    for kt in range(qt + 1):
                        ins = nc.tensor.matmul(PS["psA"][:, 65 * qt:65 * qt + 65],
                                               S["pt"][:, kt, 128 * qt:128 * (qt + 1)],
                                               S["vt"][:, kt, :], start=(kt == 0),
                                               stop=(kt == qt))
                B.i(ins, "mm")
            blk.tensor(mmpv)

            def vepv(ve):
                B.w(ve, "mm")
                for qt in range(4):
                    ve.reciprocal(S["t32a"][:, qt:qt + 1], PS["psA"][:, 65 * qt + 64:65 * qt + 65])
                for qt in range(4):
                    ve.tensor_copy(S["otm"][:, qt, :], PS["psA"][:, 65 * qt:65 * qt + 64])
                for qt in range(4):
                    ins = ve.tensor_scalar(S["otm"][:, qt, :], S["otm"][:, qt, :],
                                           S["t32a"][:, qt:qt + 1], None, ALU.mult)
                B.i(ins, "ve")
            blk.vector(vepv)

            def mmo(mm):
                B.w(mm, "ve")
                for qt in range(4):
                    ins = nc.tensor.transpose(PS["ptrB"][0:64, 128 * qt:128 * (qt + 1)],
                                              S["otm"][:, qt, :], S["identB"][:, :])
                B.i(ins, "mm")
            blk.tensor(mmo)

            def veo(ve, h=h, r0=r0):
                B.w(ve, "mm")
                for qt in range(4):
                    ins = ve.tensor_copy(S["ofm"][r0:r0 + 64, h // 2, 128 * qt:128 * (qt + 1)],
                                         PS["ptrB"][0:64, 128 * qt:128 * (qt + 1)])
                B.i(ins, "ve")
            blk.vector(veo)

        for dt in range(8):
            def mmop(mm, dt=dt):
                B.w(mm, "ve", "sc")
                for ct in range(8):
                    ins = nc.tensor.matmul(PS["psF"][:, :],
                                           S["wbuf"][:, 2048 * ct + 1024 + 128 * dt:2048 * ct + 1024 + 128 * (dt + 1)],
                                           S["ofm"][:, ct, :], start=(ct == 0), stop=(ct == 7))
                B.i(ins, "mm")
            blk.tensor(mmop)

            def scop(ve, dt=dt):
                B.w(ve, "mm")
                B.i(ve.tensor_scalar(S["tb16b"][:, 0:512], PS["psF"][:, :],
                                     S["boutS"][:, dt:dt + 1], None, ALU.add), "ve")
            blk.vector(scop)

            def veop(ve, dt=dt, seg=seg):
                B.w(ve, "sc")
                dst = S[f"res{dt}"][:, :].rearrange("p (a s r) -> p a s r", a=2, r=4)[:, seg, :, 0]
                B.i(ve.tensor_tensor(dst, dst, S["tb16b"][:, 0:512], ALU.add), "ve")
            blk.vector(veop)


def _p_output(ctx):
    nc, B, blk, S, DR = (ctx[k] for k in ["nc", "B", "blk", "S", "DR"])
    out_w, out_t = ctx["out_w"], ctx["out_t"]
    fmfull = (DR["bounce"].ap().rearrange("tk d -> (tk d)")
              .rearrange("(d s) -> d s", d=D))

    def sq1(sc):
        B.w(sc, "ve", "ld")
        for t in range(8):
            B.i(sc.dma_start(out=fmfull[128 * t:128 * (t + 1), :], in_=S[f"res{t}"][:, :]),
                "sq", 16)
    blk.scalar(sq1)

    for w in range(32):
        def sy1(sy, w=w):
            B.w(sy, "sq", "gq")
            B.i(sy.dma_start_transpose(S["eo"][:, :], fmfull[:, 128 * w:128 * (w + 1)]),
                "ld", 16)
        blk.sync(sy1)

        def gq1(gp, w=w):
            B.w(gp, "ld")
            dst = (out_w if w < 16 else out_t)
            ww = w % 16
            B.i(gp.dma_start(out=dst[128 * ww:128 * (ww + 1), :], in_=S["eo"][:, :]),
                "gq", 16)
        blk.gpsimd(gq1)

    def fin(gp):
        B.w(gp, "gq", "ld", "sq")
        gp.nop()
    blk.gpsimd(fin)


def kernel(**inputs):
    maps = _host_prep(inputs)
    nc = _build()
    res = run_bass_kernel_spmd(nc, maps, core_ids=list(range(NCORE)))
    out = np.zeros((8, 2, SEG, D), np.float32)
    for c in range(NCORE):
        out[c // 2, c % 2] = res.results[c]["out_w"]
        out[c // 2 + 4, c % 2] = res.results[c]["out_t"]
    return np.ascontiguousarray(out.reshape(8, 4096, D)[:, :, PD])


# version 12 (final)
# speedup vs baseline: 1.0316x; 1.0316x over previous
import contextlib
import numpy as np
import ml_dtypes
import concourse.bass as bass
import concourse.mybir as mybir
from concourse.bass_utils import run_bass_kernel_spmd
from concourse.bass import IndirectOffsetOnAxis

F32 = mybir.dt.float32
BF16 = mybir.dt.bfloat16
F16 = mybir.dt.float16
I32 = mybir.dt.int32
U16 = mybir.dt.uint16
AF = mybir.ActivationFunctionType
ALU = mybir.AluOpType
AX = mybir.AxisListType
bf = ml_dtypes.bfloat16

D = 1024
HID = 768
NE = 64
CAP = 256
NCORE = 8
SEG = 2048
TOK = 2 * SEG
WIN = 16384
EPC = NE // NCORE
NSLOT = EPC * CAP
NH = 16
DH = 64
M = 512
TBL_ROWS = WIN + 128
TRASH_ROW = WIN + 1
EO_PAD = 16
EO_ROWS = NSLOT + EO_PAD
GUW = 2 * HID
WROW = 8 * GUW + 6 * D
STAGE = 99


def _p4g(n):
    d = np.arange(n)
    return (d // 128) * 128 + (d % 128 % 4) * 32 + (d % 128) // 4


def _inv(p):
    q = np.empty_like(p)
    q[p] = np.arange(len(p))
    return q


PD = _p4g(D)
PI = _p4g(HID)
INV_PD = _inv(PD)
INV_PI = _inv(PI)


def _pack_u16(packed_i32):
    b = np.ascontiguousarray(packed_i32.astype(np.int32)).view(np.uint8)
    b = np.ascontiguousarray(b.reshape(*packed_i32.shape, 4)[..., 0])
    return b.view(np.uint16)


def _host_prep(inputs):
    x = inputs["L0_attn_x"]
    wqkvT = np.ascontiguousarray(inputs["L0_attn_w_in"].T[INV_PD, :], np.float32)
    woutT = np.ascontiguousarray(inputs["L0_attn_w_out"].T[:, INV_PD], np.float32)
    gateT = np.ascontiguousarray(
        inputs["L0_ffn_gate_w"].T[INV_PD, :].astype(np.float32).reshape(8, 128, NE)
        .transpose(1, 0, 2))
    b_out_p = np.ascontiguousarray(
        inputs["L0_attn_b_out"][INV_PD].astype(np.float32).reshape(8, 128).T)
    b_in_c = np.ascontiguousarray(
        inputs["L0_attn_b_in"].astype(np.float32).reshape(24, 128).T)

    gp = _pack_u16(inputs["L0_ffn_gate_packed"])[:, INV_PI, :]
    up = _pack_u16(inputs["L0_ffn_up_packed"])[:, INV_PI, :]
    gup = np.concatenate([gp, up], axis=1)
    gup_sc = np.concatenate(
        [inputs["L0_ffn_gate_scales"][:, INV_PI, :],
         inputs["L0_ffn_up_scales"][:, INV_PI, :]], axis=1).astype(np.float32)
    dn = _pack_u16(inputs["L0_ffn_down_packed"])[:, INV_PD, :]
    dn_sc = inputs["L0_ffn_down_scales"][:, INV_PD, :].astype(np.float32)

    shgup = np.concatenate([_pack_u16(inputs["L0_ffn_sh_gate_packed"])[INV_PI, :],
                            _pack_u16(inputs["L0_ffn_sh_up_packed"])[INV_PI, :]], axis=0)
    shgup_sc = np.concatenate(
        [inputs["L0_ffn_sh_gate_scales"][INV_PI, :],
         inputs["L0_ffn_sh_up_scales"][INV_PI, :]], axis=0).astype(np.float32)
    shdn = _pack_u16(inputs["L0_ffn_sh_down_packed"])[INV_PD, :]
    shdn_sc = inputs["L0_ffn_sh_down_scales"][INV_PD, :].astype(np.float32)

    p128 = np.arange(128, dtype=np.int32)
    xr = x.reshape(8, 2, SEG, D)
    maps = []
    for c in range(NCORE):
        maps.append({
            "xw": np.ascontiguousarray(xr[c // 2, c % 2], np.float32),
            "xt": np.ascontiguousarray(xr[c // 2 + 4, c % 2], np.float32),
            "wqkvT": wqkvT, "woutT": woutT, "gateT": gateT,
            "b_out_p": b_out_p, "b_in_c": b_in_c,
            "gup_pk": np.ascontiguousarray(gup[EPC * c:EPC * (c + 1)]),
            "gup_sc": np.ascontiguousarray(gup_sc[EPC * c:EPC * (c + 1)]),
            "dn_pk": np.ascontiguousarray(dn[EPC * c:EPC * (c + 1)]),
            "dn_sc": np.ascontiguousarray(dn_sc[EPC * c:EPC * (c + 1)]),
            "sh_gup_pk": shgup, "sh_gup_sc": shgup_sc,
            "sh_dn_pk": shdn, "sh_dn_sc": shdn_sc,
            "tokbase": (p128.astype(np.float32) + 2048.0 * c).reshape(128, 1),
            "slrow": (p128 * 128 + 16 * c).astype(np.int32).reshape(128, 1),
            "maskpre": np.repeat((np.arange(8) < c).astype(np.float32).reshape(1, 8),
                                 64, axis=0),
        })
    return maps


class _B:
    def __init__(self, nc, es):
        self.nc = nc
        self.v = {}
        self.h = {}
        for s in ["ld", "gq", "sq", "cc", "mm", "ve", "sc", "gp"]:
            self.h[s] = es.enter_context(nc.semaphore(s))
            self.v[s] = 0

    def w(self, eng, *sems):
        for s in sems:
            if self.v[s] > 0:
                eng.wait_ge(self.h[s], self.v[s])

    def i(self, ins, sem, k=1):
        ins.then_inc(self.h[sem], k)
        self.v[sem] += k
        return ins


def _build(stage=None, debug=False, ffnsub=99):
    if stage is None:
        stage = STAGE
    nc = bass.Bass("TRN2", target_bir_lowering=False)
    es = contextlib.ExitStack()
    B = _B(nc, es)

    A = {}
    for name, shp, dt in [
        ("xw", [SEG, D], F32), ("xt", [SEG, D], F32),
        ("wqkvT", [D, 3 * D], F32), ("woutT", [D, D], F32), ("gateT", [128, 8, NE], F32),
        ("b_out_p", [128, 8], F32), ("b_in_c", [128, 24], F32),
        ("gup_pk", [EPC, GUW, 256], U16), ("gup_sc", [EPC, GUW, 8], F32),
        ("dn_pk", [EPC, D, 192], U16), ("dn_sc", [EPC, D, 6], F32),
        ("sh_gup_pk", [GUW, 256], U16), ("sh_gup_sc", [GUW, 8], F32),
        ("sh_dn_pk", [D, 192], U16), ("sh_dn_sc", [D, 6], F32),
        ("tokbase", [128, 1], F32), ("slrow", [128, 1], I32),
        ("maskpre", [64, 8], F32),
    ]:
        A[name] = nc.dram_tensor(name, shp, dt, kind="ExternalInput").ap()
    out_w = nc.dram_tensor("out_w", [SEG, D], F32, kind="ExternalOutput").ap()
    out_t = nc.dram_tensor("out_t", [SEG, D], F32, kind="ExternalOutput").ap()
    dbg = nc.dram_tensor("dbg", [128, 4096], F32, kind="ExternalOutput").ap() if debug else None

    DR = {}
    for name, shp, dt, kw in [
        ("wstage", [EPC, 128, WROW], BF16, {}),
        ("shstage", [128, WROW], BF16, {}),
        ("attnw", [128, 2 * 16384], BF16, {}),
        ("bounce", [TOK, D], BF16, {}),
        ("lnw_in", [SEG + EO_PAD, D], BF16, {}),

        ("cnt_in", [64, 1], F32, {}),

        ("tbl_in", [TBL_ROWS, 2], F32, {}),
        ("tblsum", [WIN, 2], F32, {}),
        ("eo_in", [EO_ROWS, D], BF16, {}),
        ("sharena", [NCORE * (SEG + EO_PAD) * D // 2 + NCORE * 64
                     + NCORE * EO_ROWS * D // 2 + NCORE * TBL_ROWS * 2], F32,
         {"addr_space": "Shared"}),
    ]:
        DR[name] = nc.dram_tensor(name, shp, dt, **kw)
    _sa = DR["sharena"].ap()
    _o1 = NCORE * (SEG + EO_PAD) * D // 2
    _o2 = _o1 + NCORE * 64
    _o3 = _o2 + NCORE * EO_ROWS * D // 2
    VIEWS = {
        "lnw_out": _sa[0:_o1].bitcast(BF16).rearrange("(r d) -> r d", d=D),
        "cnt_out": _sa[_o1:_o2].rearrange("(r c) -> r c", c=1),
        "eo_out": _sa[_o2:_o3].bitcast(BF16).rearrange("(r d) -> r d", d=D),
        "gall": DR["sharena"].ap().bitcast(BF16).rearrange("(r d) -> r d", d=D),
        "eo_row0": _o2 * 2 // D,
        "tbl_ag": _sa[_o3:_o3 + NCORE * TBL_ROWS * 2],
        "tbl_rk": (lambda r, _sa=_sa, _o3=_o3:
                   _sa[_o3 + r * TBL_ROWS * 2:_o3 + r * TBL_ROWS * 2 + WIN * 2]
                   .rearrange("(p a) -> p a", p=128)),
    }

    S = {}
    def tile(name, shape, dt):
        S[name] = es.enter_context(nc.sbuf_tensor(name, shape, dt))

    for t in range(8):
        tile(f"res{t}", [128, TOK], BF16)
    for t in range(8):
        tile(f"lnc{t}", [128, 512], BF16)
    tile("ones", [128, 128], BF16)
    tile("identB", [128, 128], BF16)
    tile("identF", [128, 128], F32)
    tile("iotaE", [128, 64], F32)
    tile("cmask", [128, 4, M], BF16)
    tile("gateTs", [128, 8, NE], BF16)
    tile("binS", [128, 24], F32)
    tile("boutS", [128, 8], F32)
    tile("slrows", [128, 1], I32)
    tile("tokb", [128, 1], F32)
    tile("trashp", [128, 1], F32)
    tile("eps1", [128, 1], F32)
    tile("maskp", [64, 8], F32)
    tile("wbuf", [128, WROW], BF16)
    tile("pk", [128, 256], U16)
    tile("pksc", [128, 8], F32)
    tile("bits", [128, 1024], U16)
    tile("bita", [128, 256], U16)
    tile("bitb", [128, 256], U16)
    tile("wnat", [128, 1024], BF16)
    tile("h6", [128, 6, 512], BF16)
    tile("gb", [128, 2, D], BF16)
    tile("gbT", [128, 8, 256], BF16)
    tile("eo", [128, D], BF16)
    tile("tslice", [128, 32], F32)
    tile("tbb", [128, 2, 256], F32)
    tile("tidx", [128, 16], I32)
    tile("twgt", [128, 16], F32)
    tile("logit", [128, 16, NE], F32)
    tile("mx8", [128, 16, 8], F32)
    tile("wj4", [128, 16, 4], F32)
    tile("amask", [128, 64], BF16)
    tile("scanb", [64, SEG], F32)
    tile("cnt8", [64, 8], F32)
    tile("base", [64, 1], F32)
    tile("post", [128, 64], F32)
    tile("sig", [128, 16, 4], I32)
    tile("scat4", [128, 4, 2], F32)
    tile("i32a", [128, 64], I32)
    tile("i32b", [128, 64], I32)
    tile("i32c", [128, 64], I32)
    tile("qkv", [128, 24, M], BF16)
    tile("pt", [128, 4, M], BF16)
    tile("vt", [128, 4, DH + 1], BF16)
    tile("otm", [128, 4, DH], BF16)
    tile("ofm", [128, 8, M], BF16)
    tile("xs", [128, D], BF16)
    tile("t32a", [128, 512], F32)
    tile("t32b", [128, 512], F32)
    tile("t32c", [128, 512], F32)
    tile("t32d", [128, 512], F32)
    tile("tb16", [128, 1024], BF16)
    tile("tb16b", [128, 1024], BF16)
    tile("f32x", [128, 1024], F32)

    PS = {}
    def ptile(name, shape, dt):
        PS[name] = es.enter_context(nc.psum_tensor(name, shape, dt))
    for n in ["psA", "psB", "psC", "psD", "psE", "psF"]:
        ptile(n, [128, 512], F32)
    ptile("ptrA", [128, 1024], BF16)
    ptile("ptrB", [128, 1024], BF16)

    blk = es.enter_context(nc.Block())
    ctx = dict(nc=nc, B=B, blk=blk, A=A, DR=DR, S=S, PS=PS, V=VIEWS,
               out_w=out_w, out_t=out_t, dbg=dbg, stage=stage, ffnsub=ffnsub)
    _emit_all(ctx)
    es.close()
    return nc



def _emit_all(ctx):
    _p_consts(ctx)
    _p_xload(ctx)
    wrest = None
    if ctx["stage"] >= 1:
        wunits = _p_weights_units(ctx)
        wunits[0]()
        wrest = wunits[1:]
    if ctx["stage"] >= 2:
        _p_ffn(ctx, 0, interleave=wrest)
    elif wrest:
        for f in wrest:
            f()
    if ctx["stage"] >= 3:
        _p_attn(ctx)
    if ctx["stage"] >= 4:
        _p_ffn(ctx, 1)
    _p_output(ctx)


def _p_consts(ctx):
    nc, B, blk, A, S, DR = (ctx[k] for k in ["nc", "B", "blk", "A", "S", "DR"])

    def gp_fn(gp):
        gp.iota(S["iotaE"][:, :], [[1, 64]], base=0, channel_multiplier=0,
                allow_small_or_imprecise_dtypes=True)
        gp.memset(S["ones"][:, :], 1.0)
        gp.memset(S["vt"][:, :, :], 1.0)
        gp.memset(S["xs"][:, :], 0.0)
        gp.memset(S["eps1"][:, :], 1e-5)
        for kt, t in enumerate(["t32a", "t32b", "t32c", "t32d"]):
            gp.iota(S[t][:, :], [[1, M]], base=-128 * kt, channel_multiplier=-1,
                    allow_small_or_imprecise_dtypes=True)
        gp.iota(S["trashp"][:, :], [[1, 1]], base=WIN, channel_multiplier=1,
                allow_small_or_imprecise_dtypes=True)
        B.i(gp.iota(S["f32x"][:, 0:128], [[1, 128]], base=0, channel_multiplier=-1,
                    allow_small_or_imprecise_dtypes=True), "gp")
    blk.gpsimd(gp_fn)

    def sy_fn(sy):
        B.i(sy.dma_start(out=S["f32x"][:, 512:1024].rearrange("p (a e) -> p a e", e=NE),
                         in_=A["gateT"]), "ld", 16)
        B.i(sy.dma_start(out=S["binS"][:, :], in_=A["b_in_c"]), "ld", 16)
        B.i(sy.dma_start(out=S["boutS"][:, :], in_=A["b_out_p"]), "ld", 16)
        B.i(sy.dma_start(out=S["maskp"][:, :], in_=A["maskpre"]), "ld", 16)
        B.i(sy.dma_start(out=S["tokb"][:, :], in_=A["tokbase"]), "ld", 16)
        B.i(sy.dma_start(out=S["slrows"][:, :], in_=A["slrow"]), "ld", 16)
    blk.sync(sy_fn)

    def ve_fn(ve):
        B.w(ve, "gp", "ld")
        ve.tensor_scalar(S["identF"][:, :], S["f32x"][:, 0:128], 0.0, None, ALU.is_equal)
        for t in ["t32a", "t32b", "t32c", "t32d"]:
            ve.tensor_scalar(S[t][:, :], S[t][:, :], 0.0, None, ALU.is_ge)
        ve.tensor_copy(S["identB"][:, :], S["identF"][:, :])
        for kt, t in enumerate(["t32a", "t32b", "t32c", "t32d"]):
            ve.tensor_copy(S["cmask"][:, kt, :], S[t][:, :])
        B.i(ve.tensor_copy(S["gateTs"][:, :, :],
                           S["f32x"][:, 512:1024].rearrange("p (a e) -> p a e", e=NE)), "ve")
    blk.vector(ve_fn)

    def gq_fn(gp):
        B.w(gp, "gp")
        B.i(gp.dma_start(out=DR["lnw_in"].ap()[SEG:SEG + EO_PAD, :],
                         in_=S["xs"][0:EO_PAD, 0:D]), "gq", 16)
        B.i(gp.dma_start(out=DR["eo_in"].ap()[NSLOT:EO_ROWS, :],
                         in_=S["xs"][0:EO_PAD, 0:D]), "gq", 16)
    blk.gpsimd(gq_fn)


def _p_xload(ctx):
    nc, B, blk, A, S, DR = (ctx[k] for k in ["nc", "B", "blk", "A", "S", "DR"])
    xin = [A["xw"], A["xt"]]
    for s in range(2):
        for ch in range(16):
            first = (s == 0 and ch == 0)

            def sy(sy, s=s, ch=ch, first=first):
                B.w(sy, "ve")
                B.i(sy.dma_start(out=S["f32x"][:, :],
                                 in_=xin[s][128 * ch:128 * (ch + 1), :]), "ld", 16)
            blk.sync(sy)

            def ve(ve, first=first):
                B.w(ve, "ld")
                if not first:
                    B.w(ve, "sq")
                src = S["f32x"][:, :].rearrange("p (g q f) -> p g q f", g=8, q=32)
                dst = S["tb16"][:, :].rearrange("p (g k q) -> p g k q", g=8, k=4)
                for k in range(4):
                    ins = ve.tensor_copy(dst[:, :, k, :], src[:, :, :, k])
                B.i(ins, "ve")
            blk.vector(ve)

            def sq(sc, s=s, ch=ch):
                B.w(sc, "ve")
                B.i(sc.dma_start(
                    out=DR["bounce"].ap()[SEG * s + 128 * ch:SEG * s + 128 * (ch + 1), :],
                    in_=S["tb16"][:, 0:D]), "sq", 16)
            blk.scalar(sq)

    def sy2(sy):
        B.w(sy, "sq")
        for t in range(8):
            B.i(sy.dma_start_transpose(S[f"res{t}"][:, :],
                                       DR["bounce"].ap()[:, 128 * t:128 * (t + 1)]), "ld", 16)
    blk.sync(sy2)


def _dequant_rowtile(ctx, pk_src, sc_src, ngrp):
    nc, B, blk, S = (ctx[k] for k in ["nc", "B", "blk", "S"])
    ncol = 32 * ngrp

    def syl(sy):
        B.w(sy, "sc")
        B.i(sy.dma_start(out=S["pk"][:, 0:ncol], in_=pk_src), "ld", 16)
        B.i(sy.dma_start(out=S["pksc"][:, 0:ngrp], in_=sc_src), "ld", 16)
    blk.sync(syl)

    def vef(ve):
        B.w(ve, "ld", "sc")
        ve.tensor_scalar(S["pksc"][:, 0:ngrp], S["pksc"][:, 0:ngrp],
                         float(2 ** 14), None, ALU.mult)
        for k in range(4):
            mag_sh = 9 - 4 * k
            ve.tensor_scalar(S["bita"][:, 0:ncol], S["pk"][:, 0:ncol], 7 << (4 * k),
                             abs(mag_sh), ALU.bitwise_and,
                             ALU.logical_shift_left if mag_sh >= 0 else ALU.logical_shift_right)
            ve.tensor_scalar(S["bitb"][:, 0:ncol], S["pk"][:, 0:ncol], 8 << (4 * k),
                             12 - 4 * k, ALU.bitwise_and, ALU.logical_shift_left)
            dst = S["bits"][:, 0:128 * ngrp].rearrange("p (g k q) -> p g k q", k=4, q=32)
            ins = ve.tensor_tensor(dst[:, :, k, :],
                                   S["bita"][:, 0:ncol].rearrange("p (g q) -> p g q", q=32),
                                   S["bitb"][:, 0:ncol].rearrange("p (g q) -> p g q", q=32),
                                   ALU.bitwise_or)
        B.i(ins, "ve")
    blk.vector(vef)

    def scf(ve):
        B.w(ve, "mm")
        for g in range(ngrp):
            ins = ve.tensor_scalar(S["wnat"][:, 128 * g:128 * (g + 1)],
                                   S["bits"][:, 128 * g:128 * (g + 1)].bitcast(F16),
                                   S["pksc"][:, g:g + 1], None, ALU.mult)
        B.i(ins, "sc")
    blk.vector(scf)


def _p_weights_units(ctx):
    nc, B, blk, A, S, DR, PS = (ctx[k] for k in ["nc", "B", "blk", "A", "S", "DR", "PS"])
    units = []

    def emit_set(sname, e):
        if sname == "sh":
            pk_gu, sc_gu, pk_dn, sc_dn = (A["sh_gup_pk"], A["sh_gup_sc"],
                                          A["sh_dn_pk"], A["sh_dn_sc"])
        else:
            pk_gu, sc_gu = A["gup_pk"][e], A["gup_sc"][e]
            pk_dn, sc_dn = A["dn_pk"][e], A["dn_sc"][e]

        for kind, rt in [("gu", rt) for rt in range(12)] + [("dn", rt) for rt in range(8)]:
            ngrp = 8 if kind == "gu" else 6
            pk_src = (pk_gu if kind == "gu" else pk_dn)[128 * rt:128 * (rt + 1), :]
            sc_src = (sc_gu if kind == "gu" else sc_dn)[128 * rt:128 * (rt + 1), :]
            _dequant_rowtile(ctx, pk_src, sc_src, ngrp)

            def mmf(mm, ngrp=ngrp):
                B.w(mm, "sc", "ve")
                for cc in range(ngrp):
                    ins = nc.tensor.transpose(PS["ptrA"][:, 128 * cc:128 * (cc + 1)],
                                              S["wnat"][:, 128 * cc:128 * (cc + 1)],
                                              S["identB"][:, :])
                B.i(ins, "mm")
            blk.tensor(mmf)

            if sname == "sh":
                def vee(ve, kind=kind, rt=rt, ngrp=ngrp):
                    B.w(ve, "mm", "sq")
                    if kind == "gu":
                        dst = S["wbuf"][:, 0:8 * GUW].rearrange("p (a b) -> p a b", a=8)
                        dst = dst[:, :, 128 * rt:128 * (rt + 1)]
                        src = PS["ptrA"][:, :].rearrange("p (a q) -> p a q", a=8)
                    else:
                        dst = S["wbuf"][:, 8 * GUW:].rearrange("p (a b) -> p a b", a=6)
                        dst = dst[:, :, 128 * rt:128 * (rt + 1)]
                        src = PS["ptrA"][:, 0:768].rearrange("p (a q) -> p a q", a=6)
                    B.i(ve.tensor_copy(dst, src), "ve")
                blk.vector(vee)
            else:
                def vee(ve, ngrp=ngrp):
                    B.w(ve, "mm", "sq")
                    B.i(ve.tensor_copy(S["tb16"][:, 0:128 * ngrp],
                                       PS["ptrA"][:, 0:128 * ngrp]), "ve")
                blk.vector(vee)

                def sqs(sc, kind=kind, rt=rt, ngrp=ngrp, e=e):
                    B.w(sc, "ve")
                    wst = DR["wstage"].ap()[e]
                    if kind == "gu":
                        dst = wst[:, 0:8 * GUW].rearrange("p (a b) -> p a b", a=8)
                    else:
                        dst = wst[:, 8 * GUW:].rearrange("p (a b) -> p a b", a=6)
                    dst = dst[:, :, 128 * rt:128 * (rt + 1)]
                    B.i(sc.dma_start(out=dst,
                                     in_=S["tb16"][:, 0:128 * ngrp].rearrange(
                                         "p (a b) -> p a b", b=128)), "sq", 16)
                blk.scalar(sqs)

        if sname == "sh":
            def sys(sc):
                B.w(sc, "ve")
                B.i(sc.dma_start(out=DR["shstage"].ap(), in_=S["wbuf"][:, :]), "sq", 16)
            blk.scalar(sys)

    units.append(lambda: emit_set("sh", None))
    for e in range(EPC):
        units.append(lambda e=e: emit_set("ex", e))

    def _cast_store(pairs):
        for sap, dap in pairs:
            def syj(sy, sap=sap):
                B.w(sy, "ve")
                B.i(sy.dma_start(out=S["f32x"][:, :], in_=sap), "ld", 16)
            blk.sync(syj)

            def vej(ve):
                B.w(ve, "ld", "sq")
                B.i(ve.tensor_copy(S["tb16"][:, 0:1024], S["f32x"][:, :]), "ve")
            blk.vector(vej)

            def sqj(sc, dap=dap):
                B.w(sc, "ve")
                B.i(sc.dma_start(out=dap, in_=S["tb16"][:, 0:1024]), "sq", 16)
            blk.scalar(sqj)

    pairs = []
    for a in range(8):
        for u in range(2):
            pairs.append((A["wqkvT"][128 * a:128 * (a + 1), 1024 * u:1024 * (u + 1)],
                          DR["attnw"].ap()[:, 2048 * a + 1024 * u:2048 * a + 1024 * (u + 1)]))
    for a in range(8):
        pairs.append((A["wqkvT"][128 * a:128 * (a + 1), 2048:3072],
                      DR["attnw"].ap()[:, 16384 + 2048 * a:16384 + 2048 * a + 1024]))
    for a in range(8):
        pairs.append((A["woutT"][128 * a:128 * (a + 1), :],
                      DR["attnw"].ap()[:, 16384 + 2048 * a + 1024:16384 + 2048 * (a + 1)]))
    units.append(lambda: _cast_store(pairs[0:16]))
    units.append(lambda: _cast_store(pairs[16:32]))
    return units


def _ln_chunk(ctx, ch, src, dstpref="lnc", width=512):
    nc, B, blk, S, PS = (ctx[k] for k in ["nc", "B", "blk", "S", "PS"])
    cols = slice(512 * ch, 512 * ch + width)

    def mm1(mm):
        B.w(mm, "ve", "sc")
        for t in range(8):
            ins = nc.tensor.matmul(PS["psD"][:, 0:width], S["ones"][:, :],
                                   S[src + str(t)][:, cols], start=(t == 0), stop=(t == 7))
        B.i(ins, "mm")
    blk.tensor(mm1)

    for t in range(8):
        def scx(ve, t=t):
            B.w(ve, "mm")
            B.i(ve.tensor_tensor(S["tb16b"][:, 0:width], S[src + str(t)][:, cols],
                                 S[src + str(t)][:, cols], ALU.mult), "sc")
        blk.vector(scx)

        def mm2(mm, t=t):
            B.w(mm, "sc")
            B.i(nc.tensor.matmul(PS["psE"][:, 0:width], S["ones"][:, :],
                                 S["tb16b"][:, 0:width], start=(t == 0), stop=(t == 7)), "mm")
        blk.tensor(mm2)

    def ve1(ve):
        B.w(ve, "mm")
        ve.tensor_scalar(S["t32a"][:, 0:width], PS["psD"][:, 0:width], 1.0 / D, None, ALU.mult)
        ve.tensor_scalar(S["t32b"][:, 0:width], PS["psE"][:, 0:width], 1.0 / D, None, ALU.mult)
        ve.tensor_tensor(S["t32c"][:, 0:width], S["t32a"][:, 0:width], S["t32a"][:, 0:width],
                         ALU.mult)
        ve.tensor_tensor(S["t32b"][:, 0:width], S["t32b"][:, 0:width],
                         S["t32c"][:, 0:width], ALU.subtract)
        ve.tensor_scalar(S["t32b"][:, 0:width], S["t32b"][:, 0:width], 1e-5,
                         None, ALU.add)
        B.i(ve.reciprocal(S["t32c"][:, 0:width], S["t32b"][:, 0:width]), "ve")
    blk.vector(ve1)

    def sc1(sc):
        B.w(sc, "ve")
        B.i(sc.activation(out=S["t32d"][:, 0:width], in_=S["t32c"][:, 0:width],
                          func=AF.Sqrt), "sc")
    blk.scalar(sc1)

    def ve2(ve):
        B.w(ve, "sc", "mm", "sq")
        for t in range(8):
            ve.tensor_tensor(S["tb16b"][:, 0:width], S[src + str(t)][:, cols],
                             S["t32a"][:, 0:width], ALU.subtract)
            ins = ve.tensor_tensor(S[dstpref + str(t)][:, 0:width], S["tb16b"][:, 0:width],
                                   S["t32d"][:, 0:width], ALU.mult)
        B.i(ins, "ve")
    blk.vector(ve2)


def _fmv(ctx):
    return (ctx["DR"]["bounce"].ap().rearrange("tk d -> (tk d)")[0:D * SEG]
            .rearrange("(d s) -> d s", d=D))


def _stage_and_route(ctx, call, ch):
    nc, B, blk, A, S, DR, PS = (ctx[k] for k in ["nc", "B", "blk", "A", "S", "DR", "PS"])
    fmv = _fmv(ctx)

    def sq2(sc):
        B.w(sc, "ve", "ld")
        for t in range(8):
            B.i(sc.dma_start(out=fmv[128 * t:128 * (t + 1), 512 * ch:512 * (ch + 1)],
                             in_=S[f"lnc{t}"][:, :]), "sq", 16)
    blk.scalar(sq2)

    for sub in range(4):
        wc = 4 * ch + sub
        tcols = slice(128 * sub, 128 * (sub + 1))

        def mmr(mm, tcols=tcols):
            B.w(mm, "ve", "sc")
            for t in range(8):
                ins = nc.tensor.matmul(PS["psF"][:, 0:NE], S[f"lnc{t}"][:, tcols],
                                       S["gateTs"][:, t, :], start=(t == 0), stop=(t == 7))
            B.i(ins, "mm")
        blk.tensor(mmr)

        def ver(ve, wc=wc):
            B.w(ve, "mm")
            ve.tensor_copy(S["logit"][:, wc, :], PS["psF"][:, 0:NE])
            ve.max(S["mx8"][:, wc, :], S["logit"][:, wc, :])
            B.i(ve.tensor_scalar(S["t32a"][:, 0:1], S["mx8"][:, wc, 0:1], -1.0, None,
                                 ALU.mult), "ve")
        blk.vector(ver)

        def scr(sc, wc=wc):
            B.w(sc, "ve")
            B.i(sc.activation(out=S["wj4"][:, wc, :], in_=S["mx8"][:, wc, 0:4], func=AF.Exp,
                              bias=S["t32a"][:, 0:1], scale=1.0), "sc")
        blk.scalar(scr)

        def ver2(ve, wc=wc):
            B.w(ve, "sc")
            ve.reduce_sum(S["t32b"][:, 0:1], S["wj4"][:, wc, :], axis=AX.X)
            ve.tensor_scalar(S["amask"][:, :], S["logit"][:, wc, :], S["mx8"][:, wc, 3:4],
                             None, ALU.is_ge)
            ve.reciprocal(S["t32c"][:, 0:1], S["t32b"][:, 0:1])
            ve.tensor_copy(S["t32a"][:, 0:64], S["logit"][:, wc, :])
            B.i(ve.tensor_scalar(S["wj4"][:, wc, :], S["wj4"][:, wc, :], S["t32c"][:, 0:1],
                                 None, ALU.mult), "ve")
        blk.vector(ver2)

        def mmt(mm):
            B.w(mm, "ve")
            B.i(nc.tensor.transpose(PS["ptrB"][0:64, 0:128], S["amask"][:, :],
                                    S["identB"][:, :]), "mm")
        blk.tensor(mmt)

        def vet(ve, wc=wc):
            B.w(ve, "mm")
            B.i(ve.tensor_copy(S["scanb"][0:64, 128 * wc:128 * (wc + 1)],
                               PS["ptrB"][0:64, 0:128]), "ve")
        blk.vector(vet)


def _shared_chunk(ctx, ch):
    nc, B, blk, S, PS = (ctx[k] for k in ["nc", "B", "blk", "S", "PS"])
    cols = slice(512 * ch, 512 * (ch + 1))

    for ip in range(6):
        def mmg(mm, ip=ip):
            B.w(mm, "ve", "sc")
            for t in range(8):
                nc.tensor.matmul(PS["psA"][:, :], S["wbuf"][:, GUW * t + 128 * ip:GUW * t + 128 * (ip + 1)],
                                 S[f"lnc{t}"][:, :], start=(t == 0), stop=(t == 7))
            for t in range(8):
                ins = nc.tensor.matmul(PS["psB"][:, :],
                                       S["wbuf"][:, GUW * t + 128 * (ip + 6):GUW * t + 128 * (ip + 7)],
                                       S[f"lnc{t}"][:, :], start=(t == 0), stop=(t == 7))
            B.i(ins, "mm")
        blk.tensor(mmg)

        def scg(sc):
            B.w(sc, "mm")
            B.i(sc.activation(out=S["tb16b"][:, 0:512], in_=PS["psA"][:, :], func=AF.Silu), "sc")
        blk.scalar(scg)

        def veg(ve, ip=ip):
            B.w(ve, "sc")
            B.i(ve.tensor_tensor(S["h6"][:, ip, :], S["tb16b"][:, 0:512], PS["psB"][:, :],
                                 ALU.mult), "ve")
        blk.vector(veg)

    for dt in range(8):
        def mmd(mm, dt=dt):
            B.w(mm, "ve", "sc")
            for ip in range(6):
                ins = nc.tensor.matmul(PS["psC"][:, :],
                                       S["wbuf"][:, 8 * GUW + D * ip + 128 * dt:8 * GUW + D * ip + 128 * (dt + 1)],
                                       S["h6"][:, ip, :], start=(ip == 0), stop=(ip == 5))
            B.i(ins, "mm")
        blk.tensor(mmd)

        def ved(ve, dt=dt):
            B.w(ve, "mm")
            B.i(ve.scalar_tensor_tensor(S[f"res{dt}"][:, cols], PS["psC"][:, :], 0.5,
                                        S[f"res{dt}"][:, cols], ALU.mult, ALU.add), "ve")
        blk.vector(ved)


def _stage_transpose_ag(ctx, call):
    nc, B, blk, S, DR = (ctx[k] for k in ["nc", "B", "blk", "S", "DR"])
    fmv = _fmv(ctx)
    for w in range(16):
        def sy1(sy, w=w):
            B.w(sy, "sq", "gq")
            B.i(sy.dma_start_transpose(S["eo"][:, :], fmv[:, 128 * w:128 * (w + 1)]), "ld", 16)
        blk.sync(sy1)

        def gq1(gp, w=w):
            B.w(gp, "ld", "cc")
            B.i(gp.dma_start(out=DR["lnw_in"].ap()[128 * w:128 * (w + 1), :],
                             in_=S["eo"][:, :]), "gq", 16)
        blk.gpsimd(gq1)

    def gcc(gp):
        B.w(gp, "gq")
        B.i(gp.collective_compute("AllGather", ALU.bypass, replica_groups=[list(range(NCORE))],
                                  ins=[DR["lnw_in"].ap().opt()],
                                  outs=[ctx["V"]["lnw_out"].opt()]), "cc")
    blk.gpsimd(gcc)


def _scan_phase(ctx, call):
    nc, B, blk, S, DR, PS = (ctx[k] for k in ["nc", "B", "blk", "S", "DR", "PS"])

    def ve1(ve):
        B.i(ve.reduce_sum(S["cnt8"][0:64, 0:1], S["scanb"][0:64, :], axis=AX.X), "ve")
    blk.vector(ve1)

    def gq1(gp):
        B.w(gp, "ve")
        B.i(gp.dma_start(out=DR["cnt_in"].ap(), in_=S["cnt8"][0:64, 0:1]), "gq", 16)
        B.w(gp, "gq")
        B.i(gp.collective_compute("AllGather", ALU.bypass, replica_groups=[list(range(NCORE))],
                                  ins=[DR["cnt_in"].ap().opt()],
                                  outs=[ctx["V"]["cnt_out"].opt()]), "cc")
        B.w(gp, "cc")
        B.i(gp.dma_start(out=S["t32a"][0:8, 0:64],
                         in_=ctx["V"]["cnt_out"].rearrange("(k e) c -> k (e c)", e=64)), "gq", 16)
    blk.gpsimd(gq1)

    def mmc(mm):
        B.w(mm, "gq", "ve")
        B.i(nc.tensor.transpose(PS["psF"][0:64, 0:8], S["t32a"][0:8, 0:64],
                                S["identF"][0:8, 0:8]), "mm")
    blk.tensor(mmc)

    def vec8(ve):
        B.w(ve, "mm")
        B.i(ve.tensor_copy(S["cnt8"][0:64, 0:8], PS["psF"][0:64, 0:8]), "ve")
    blk.vector(vec8)

    def ve2(ve):
        B.w(ve, "gq")
        ve.memset(S["base"][0:64, :], 0.0)
        for k in range(NCORE):
            ve.scalar_tensor_tensor(S["base"][0:64, :], S["cnt8"][0:64, k:k + 1],
                                    S["maskp"][0:64, k:k + 1], S["base"][0:64, :],
                                    ALU.mult, ALU.add)
        ve.tensor_tensor_scan(S["scanb"][0:64, :], S["scanb"][0:64, :], S["scanb"][0:64, :],
                              0.0, ALU.add, ALU.bypass)
        B.i(ve.tensor_scalar(S["scanb"][0:64, :], S["scanb"][0:64, :], S["base"][0:64, 0:1],
                             None, ALU.add), "ve")
    blk.vector(ve2)


def _sigma_scatter(ctx, call):
    nc, B, blk, A, S, DR, PS = (ctx[k] for k in ["nc", "B", "blk", "A", "S", "DR", "PS"])

    for wc in range(16):
        def mm1(mm, wc=wc):
            B.w(mm, "ve")
            B.i(nc.tensor.transpose(PS["psF"][:, 0:64],
                                    S["scanb"][0:64, 128 * wc:128 * (wc + 1)],
                                    S["identF"][0:64, 0:64]), "mm")
        blk.tensor(mm1)

        def vsig(ve, wc=wc):
            B.w(ve, "mm", "gq")
            ve.tensor_copy(S["post"][:, :], PS["psF"][:, 0:64])
            for j in range(4):
                ve.tensor_scalar(S["t32a"][:, 0:64], S["logit"][:, wc, :],
                                 S["mx8"][:, wc, j:j + 1], None, ALU.is_equal)
                ve.tensor_tensor(S["t32b"][:, 0:64], S["post"][:, :], S["t32a"][:, 0:64],
                                 ALU.mult)
                ve.reduce_sum(S["t32c"][:, j:j + 1], S["t32b"][:, 0:64], axis=AX.X)
                ve.tensor_tensor(S["t32b"][:, 0:64], S["iotaE"][:, :], S["t32a"][:, 0:64],
                                 ALU.mult)
                ve.reduce_sum(S["t32d"][:, j:j + 1], S["t32b"][:, 0:64], axis=AX.X)
            ia, ib, ic = S["i32a"], S["i32b"], S["i32c"]
            ve.tensor_scalar(S["t32c"][:, 0:4], S["t32c"][:, 0:4], -1.0, None, ALU.add)
            ve.tensor_scalar(S["scat4"][:, 0, 0:1], S["tokb"][:, :], float(128 * wc),
                             None, ALU.add)
            ve.tensor_copy(ia[:, 0:4], S["t32d"][:, 0:4])
            ve.tensor_scalar(S["t32a"][:, 0:4], S["t32c"][:, 0:4], float(CAP), None,
                             ALU.is_ge)
            ve.tensor_copy(S["scat4"][:, 0, 1:2], S["wj4"][:, wc, 0:1])
            ve.tensor_copy(ib[:, 0:4], S["t32c"][:, 0:4])
            ve.tensor_scalar(ia[:, 4:8], ia[:, 0:4], 8, None,
                             ALU.logical_shift_left)
            ve.tensor_scalar(S["t32b"][:, 0:4], S["t32a"][:, 0:4], -1.0, 1.0,
                             ALU.mult, ALU.add)
            ve.tensor_scalar(ia[:, 8:12], ia[:, 0:4], 3, 4,
                             ALU.logical_shift_right, ALU.logical_shift_left)
            ve.tensor_copy(ic[:, 0:4], S["t32a"][:, 0:4])
            ve.tensor_tensor(ib[:, 4:8], ia[:, 4:8], ib[:, 0:4], ALU.add)
            ve.tensor_copy(ic[:, 4:8], S["t32b"][:, 0:4])
            ve.tensor_scalar(S["scat4"][:, 1, 0:1], S["tokb"][:, :], float(128 * wc),
                             None, ALU.add)
            ve.tensor_scalar(ic[:, 8:12], ic[:, 0:4], 11, None,
                             ALU.logical_shift_left)
            ve.tensor_scalar(ia[:, 12:16], ib[:, 4:8], 127, 7,
                             ALU.bitwise_and, ALU.logical_shift_left)
            ve.tensor_tensor(ia[:, 16:20], ia[:, 8:12], ib[:, 4:8], ALU.add)
            ve.tensor_scalar(ia[:, 20:24], ib[:, 4:8], 7, None,
                             ALU.logical_shift_right)
            ve.tensor_scalar(S["t32c"][:, 4:8], S["t32a"][:, 0:4], S["trashp"][:, 0:1],
                             None, ALU.mult)
            ve.tensor_tensor(ia[:, 24:28], ia[:, 16:20], ic[:, 4:8], ALU.mult)
            ve.tensor_tensor(ia[:, 28:32], ia[:, 12:16], ia[:, 20:24], ALU.add)
            ve.tensor_copy(S["scat4"][:, 1, 1:2], S["wj4"][:, wc, 1:2])
            ve.tensor_copy(ic[:, 12:16], S["t32c"][:, 4:8])
            ve.tensor_scalar(S["scat4"][:, 2, 0:1], S["tokb"][:, :], float(128 * wc),
                             None, ALU.add)
            ve.tensor_tensor(S["sig"][:, wc, :], ia[:, 24:28], ic[:, 8:12], ALU.add)
            ve.tensor_tensor(ia[:, 32:36], ia[:, 28:32], ic[:, 4:8], ALU.mult)
            ve.tensor_copy(S["scat4"][:, 2, 1:2], S["wj4"][:, wc, 2:3])
            ve.tensor_scalar(S["scat4"][:, 3, 0:1], S["tokb"][:, :], float(128 * wc),
                             None, ALU.add)
            ve.tensor_tensor(ib[:, 4:8], ia[:, 32:36], ic[:, 12:16], ALU.add)
            B.i(ve.tensor_copy(S["scat4"][:, 3, 1:2], S["wj4"][:, wc, 3:4]), "ve")
        blk.vector(vsig)

        for j in range(4):
            def gsc(gp, j=j):
                B.w(gp, "ve")
                B.i(gp.indirect_dma_start(DR["tbl_in"].ap(),
                                          IndirectOffsetOnAxis(S["i32b"][:, 4 + j:5 + j], 0),
                                          S["scat4"][:, j, :], None,
                                          compute_op=ALU.add), "gq", 16)
            blk.gpsimd(gsc)

    def gar(gp):
        B.w(gp, "gq")
        B.i(gp.collective_compute("AllGather", ALU.bypass, replica_groups=[list(range(NCORE))],
                                  ins=[DR["tbl_in"].ap().opt()],
                                  outs=[ctx["V"]["tbl_ag"].opt()]), "cc")
    blk.gpsimd(gar)

    def syt0(sy):
        B.w(sy, "cc")
        B.i(sy.dma_start(out=S["tbb"][:, 0, :], in_=ctx["V"]["tbl_rk"](0)), "ld", 16)
        B.i(sy.dma_start(out=S["tbb"][:, 1, :], in_=ctx["V"]["tbl_rk"](1)), "ld", 16)
    blk.sync(syt0)
    for r in range(2, NCORE + 1):
        def vetk(ve):
            B.w(ve, "ld")
            B.i(ve.tensor_tensor(S["tbb"][:, 0, :], S["tbb"][:, 0, :], S["tbb"][:, 1, :],
                                 ALU.add), "ve")
        blk.vector(vetk)
        if r < NCORE:
            def sytk(sy, r=r):
                B.w(sy, "ve")
                B.i(sy.dma_start(out=S["tbb"][:, 1, :], in_=ctx["V"]["tbl_rk"](r)), "ld", 16)
            blk.sync(sytk)

    def gsum(gp):
        B.w(gp, "ve")
        B.i(gp.dma_start(out=DR["tblsum"].ap().rearrange("(p a) c -> p (a c)", p=128),
                         in_=S["tbb"][:, 0, :]), "gq", 16)
        B.w(gp, "gq")
        B.i(gp.indirect_dma_start(S["tslice"][:, :], None, DR["tblsum"].ap(),
                                  IndirectOffsetOnAxis(S["slrows"][:, :], 0)), "gq", 16)
    blk.gpsimd(gsum)

    def vtw(ve):
        B.w(ve, "gq")
        tsv = S["tslice"][:, :].rearrange("p (n c) -> p n c", c=2)
        ve.tensor_copy(S["tidx"][:, :], tsv[:, :, 0])
        ve.tensor_scalar(S["twgt"][:, :], tsv[:, :, 1], 0.5, None, ALU.mult)
        ve.tensor_copy(S["t32a"][:, 0:16], tsv[:, :, 1])
        ve.tensor_scalar(S["i32a"][:, 0:16], S["tidx"][:, :], 11, 4,
                         ALU.logical_shift_right, ALU.logical_shift_left)
        ve.tensor_copy(S["t32b"][:, 0:16], S["t32a"][:, 0:16])
        ve.tensor_copy(S["t32c"][:, 0:16], S["t32a"][:, 0:16])
        B.i(ve.tensor_tensor(S["tidx"][:, :], S["tidx"][:, :], S["i32a"][:, 0:16],
                             ALU.add), "ve")
    blk.vector(vtw)


def _dbg_dump(ctx):
    nc, B, blk, S, DR, PS = (ctx[k] for k in ["nc", "B", "blk", "S", "DR", "PS"])
    dbg = ctx["dbg"]

    def gd(gp):
        B.w(gp, "ve", "gq", "cc", "mm", "sc")
        B.i(gp.dma_start(out=dbg[:, 0:32], in_=S["tslice"][:, :]), "gq", 16)
        B.i(gp.dma_start(out=dbg[:, 32:96].bitcast(I32).rearrange("p (a b) -> p a b", b=4),
                         in_=S["sig"][:, :, :]), "gq", 16)
        B.i(gp.dma_start(out=dbg[0:64, 96:104], in_=S["cnt8"][0:64, :]), "gq", 16)
        B.i(gp.dma_start(out=dbg[0:64, 112:120], in_=S["cnt8"][0:64, :]), "gq", 16)
        B.i(gp.dma_start(out=dbg[0:8, 120:184], in_=S["t32a"][0:8, 0:64]), "gq", 16)
        B.i(gp.dma_start(out=dbg[0:64, 184:192], in_=ctx["V"]["cnt_out"].rearrange("(k e) c -> e k c", e=64)[0:64, 0:8, 0], allow_slow_non_contiguous=True), "gq", 16)
        B.i(gp.dma_start(out=dbg[:, 128:192], in_=S["logit"][:, 0, :]), "gq", 16)
        B.i(gp.dma_start(out=dbg[:, 192:200], in_=S["mx8"][:, 0, :]), "gq", 16)
        B.i(gp.dma_start(out=dbg[0:64, 1024:3072], in_=S["scanb"][0:64, :]), "gq", 16)
        B.i(gp.dma_start(out=dbg[:, 3072:3584], in_=S["lnc0"][:, :]), "gq", 16)
        B.i(gp.dma_start(out=dbg[:, 232:1000].bitcast(BF16)[:, 0:1536], in_=S["wbuf"][:, 0:1536]), "gq", 16)
        B.i(gp.dma_start(out=dbg[:, 3584:4096].bitcast(BF16)[:, 0:1024],
                         in_=S["wbuf"][:, 8 * GUW:8 * GUW + 1024]), "gq", 16)
        B.i(gp.dma_start(out=dbg[:, 200:216], in_=S["twgt"][:, :]), "gq", 16)
        B.i(gp.dma_start(out=dbg[:, 216:232].bitcast(I32), in_=S["tidx"][:, :]), "gq", 16)
        B.w(gp, "gq")
    blk.gpsimd(gd)


def _experts(ctx, call):
    nc, B, blk, S, DR, PS = (ctx[k] for k in ["nc", "B", "blk", "S", "DR", "PS"])

    for e in range(EPC):
        def syw(sy, e=e):
            B.w(sy, "mm", "sq")
            B.i(sy.dma_start(out=S["wbuf"][:, :], in_=DR["wstage"].ap()[e]), "ld", 16)
        blk.sync(syw)

        for s in range(2):
            def gg(gp, e=e, s=s):
                B.w(gp, "ve")
                B.w(gp, "mm")
                B.i(gp.indirect_dma_start(S["gb"][:, s, :], None, ctx["V"]["lnw_out"],
                                          IndirectOffsetOnAxis(S["tidx"][:, 2 * e + s:2 * e + s + 1], 0)),
                    "gq", 16)
            blk.gpsimd(gg)

        for s in range(2):
            def mmt(mm, s=s):
                B.w(mm, "gq", "ve")
                for cc in range(8):
                    ins = nc.tensor.transpose(PS["ptrB"][:, 128 * cc:128 * (cc + 1)],
                                              S["gb"][:, s, 128 * cc:128 * (cc + 1)],
                                              S["identB"][:, :])
                B.i(ins, "mm")
            blk.tensor(mmt)

            def vet(ve, s=s):
                B.w(ve, "mm")
                B.i(ve.tensor_copy(S["gbT"][:, :, 128 * s:128 * (s + 1)],
                                   PS["ptrB"][:, :].rearrange("p (a q) -> p a q", a=8)), "ve")
            blk.vector(vet)

        for ip in range(6):
            def mmg(mm, ip=ip):
                B.w(mm, "ve", "sc", "ld")
                for t in range(8):
                    nc.tensor.matmul(PS["psA"][:, 0:256],
                                     S["wbuf"][:, GUW * t + 128 * ip:GUW * t + 128 * (ip + 1)],
                                     S["gbT"][:, t, :], start=(t == 0), stop=(t == 7))
                for t in range(8):
                    ins = nc.tensor.matmul(PS["psB"][:, 0:256],
                                           S["wbuf"][:, GUW * t + 128 * (ip + 6):GUW * t + 128 * (ip + 7)],
                                           S["gbT"][:, t, :], start=(t == 0), stop=(t == 7))
                B.i(ins, "mm")
            blk.tensor(mmg)

            def scg(sc):
                B.w(sc, "mm")
                B.i(sc.activation(out=S["tb16b"][:, 0:256], in_=PS["psA"][:, 0:256],
                                  func=AF.Silu), "sc")
            blk.scalar(scg)

            def veg(ve, ip=ip):
                B.w(ve, "sc")
                B.i(ve.tensor_tensor(S["h6"][:, ip, 0:256], S["tb16b"][:, 0:256],
                                     PS["psB"][:, 0:256], ALU.mult), "ve")
            blk.vector(veg)

        for s in range(2):
            for dh in range(2):
                def mmd(mm, s=s, dh=dh):
                    B.w(mm, "ve", "sc")
                    for ip in range(6):
                        ins = nc.tensor.matmul(
                            PS["psC"][:, :],
                            S["h6"][:, ip, 128 * s:128 * (s + 1)],
                            S["wbuf"][:, 8 * GUW + D * ip + 512 * dh:8 * GUW + D * ip + 512 * (dh + 1)],
                            start=(ip == 0), stop=(ip == 5))
                    B.i(ins, "mm")
                blk.tensor(mmd)

                def ved(ve, e=e, s=s, dh=dh):
                    B.w(ve, "mm", "gq")
                    B.i(ve.tensor_scalar(S["eo"][:, 512 * dh:512 * (dh + 1)], PS["psC"][:, :],
                                         S["twgt"][:, 2 * e + s:2 * e + s + 1], None,
                                         ALU.mult), "ve")
                blk.vector(ved)

            def geo(gp, e=e, s=s):
                B.w(gp, "ve", "cc")
                B.i(gp.dma_start(
                    out=DR["eo_in"].ap()[0:NSLOT, :].rearrange("(n p) d -> p n d", p=128)[:, 2 * e + s],
                    in_=S["eo"][:, :]), "gq", 16)
            blk.gpsimd(geo)


def _combine(ctx, call):
    nc, B, blk, S, DR, PS = (ctx[k] for k in ["nc", "B", "blk", "S", "DR", "PS"])

    def gcc(gp):
        B.w(gp, "gq")
        B.i(gp.collective_compute("AllGather", ALU.bypass, replica_groups=[list(range(NCORE))],
                                  ins=[DR["eo_in"].ap().opt()],
                                  outs=[ctx["V"]["eo_out"].opt()]), "cc")
        B.w(gp, "cc")
    blk.gpsimd(gcc)

    qk = S["qkv"][:, :, :].rearrange("p a m -> p (a m)")
    gq_after = {}

    def emit_g(wc):
        for j in range(4):
            def gx(gp, wc=wc, j=j):
                B.w(gp, "ve")
                s = 4 * (wc % 2) + j
                B.i(gp.indirect_dma_start(qk[:, 1024 * s:1024 * (s + 1)], None,
                                          ctx["V"]["gall"],
                                          IndirectOffsetOnAxis(S["sig"][:, wc, j:j + 1], 0),
                                          element_offset=ctx["V"]["eo_row0"] * D),
                    "gq", 16)
            blk.gpsimd(gx)
        gq_after[wc] = B.v["gq"]

    emit_g(0)
    for wc in range(16):
        if wc + 1 < 16:
            emit_g(wc + 1)

        def vsum(ve, wc=wc):
            ve.wait_ge(B.h["gq"], gq_after[wc])
            b = 4 * (wc % 2)
            ve.tensor_tensor(S["xs"][:, :], qk[:, 1024 * b:1024 * (b + 1)],
                             qk[:, 1024 * (b + 1):1024 * (b + 2)], ALU.add)
            ve.tensor_tensor(S["tb16b"][:, 0:D], qk[:, 1024 * (b + 2):1024 * (b + 3)],
                             qk[:, 1024 * (b + 3):1024 * (b + 4)], ALU.add)
            B.i(ve.tensor_tensor(S["xs"][:, :], S["xs"][:, :], S["tb16b"][:, 0:D],
                                 ALU.add), "ve")
        blk.vector(vsum)

        def mmx(mm):
            B.w(mm, "ve")
            for cc in range(8):
                ins = nc.tensor.transpose(PS["ptrA"][:, 128 * cc:128 * (cc + 1)],
                                          S["xs"][:, 128 * cc:128 * (cc + 1)],
                                          S["identB"][:, :])
            B.i(ins, "mm")
        blk.tensor(mmx)

        def vadd(ve, wc=wc):
            B.w(ve, "mm")
            for t in range(8):
                ins = ve.scalar_tensor_tensor(S[f"res{t}"][:, 128 * wc:128 * (wc + 1)],
                                              PS["ptrA"][:, 128 * t:128 * (t + 1)], 1.0,
                                              S[f"res{t}"][:, 128 * wc:128 * (wc + 1)],
                                              ALU.mult, ALU.add)
            B.i(ins, "ve")
        blk.vector(vadd)


def _p_ffn(ctx, call, interleave=None):
    nc, B, blk, S, DR = (ctx[k] for k in ["nc", "B", "blk", "S", "DR"])
    sub = ctx.get("ffnsub", 99)
    units = []

    def u_pre():
        def gz(gp):
            B.w(gp, "cc")
            B.w(gp, "ve")
            B.i(gp.memset(S["t32d"][:, 0:2 * (TBL_ROWS // 128)], 0.0), "gp")
            tv = DR["tbl_in"].ap().rearrange("(p a) c -> p a c", p=128)
            B.i(gp.dma_start(out=tv,
                             in_=S["t32d"][:, 0:2 * (TBL_ROWS // 128)].rearrange(
                                 "p (a c) -> p a c", c=2)), "gq", 16)
        blk.gpsimd(gz)

        if call == 1:
            def syw(sy):
                B.w(sy, "mm", "sq")
                B.i(sy.dma_start(out=S["wbuf"][:, :], in_=DR["shstage"].ap()), "ld", 16)
            blk.sync(syw)
    units.append(u_pre)

    def mk_ch(ch):
        def u_ch():
            _ln_chunk(ctx, ch, "res")
            if ch < 4 and sub >= 1:
                _stage_and_route(ctx, call, ch)
            if sub >= 2:
                _shared_chunk(ctx, ch)
        return u_ch
    for ch in range(4):
        units.append(mk_ch(ch))

    def u_sta():
        if sub >= 3:
            _stage_transpose_ag(ctx, call)
        if sub >= 4:
            _scan_phase(ctx, call)
    units.append(u_sta)

    def u_sig():
        if sub >= 5:
            _sigma_scatter(ctx, call)
            if ctx["dbg"] is not None and call == 0:
                _dbg_dump(ctx)
    units.append(u_sig)

    for ch in range(4, 8):
        units.append(mk_ch(ch))

    other = list(interleave) if interleave else []
    i = j = 0
    while i < len(units) or j < len(other):
        if i < len(units):
            units[i]()
            i += 1
        if j < len(other):
            other[j]()
            j += 1

    if sub >= 6:
        _experts(ctx, call)
    if sub >= 7:
        _combine(ctx, call)


def _p_attn(ctx):
    nc, B, blk, S, DR, PS = (ctx[k] for k in ["nc", "B", "blk", "S", "DR", "PS"])

    for seg in range(2):
        B0 = SEG * seg

        def sywA(sy):
            B.w(sy, "mm", "sq")
            B.i(sy.dma_start(out=S["wbuf"][:, 0:16384], in_=DR["attnw"].ap()[:, 0:16384]),
                "ld", 16)
        blk.sync(sywA)

        def vec(ve, B0=B0):
            B.w(ve, "ve")
            for t in range(8):
                src = S[f"res{t}"][:, :].rearrange("p (a s r) -> p a s r", a=2, r=4)[:, seg, :, 0]
                ins = ve.tensor_copy(S[f"lnc{t}"][:, :], src)
            B.i(ins, "ve")
        blk.vector(vec)
        _ln_chunk(ctx, 0, "lnc")

        for ec in range(16):
            def mmq(mm, ec=ec):
                B.w(mm, "ve", "sc", "ld")
                for t in range(8):
                    ins = nc.tensor.matmul(PS["psA"][:, :],
                                           S["wbuf"][:, 2048 * t + 128 * ec:2048 * t + 128 * (ec + 1)],
                                           S[f"lnc{t}"][:, :], start=(t == 0), stop=(t == 7))
                B.i(ins, "mm")
            blk.tensor(mmq)

            def scq(ve, ec=ec):
                B.w(ve, "mm")
                B.i(ve.tensor_scalar(S["qkv"][:, ec, :], PS["psA"][:, :],
                                     S["binS"][:, ec:ec + 1], None, ALU.add), "ve")
            blk.vector(scq)

        def sywB(sy):
            B.w(sy, "mm", "sc", "sq")
            B.i(sy.dma_start(out=S["wbuf"][:, 0:16384],
                             in_=DR["attnw"].ap()[:, 16384:2 * 16384]), "ld", 16)
        blk.sync(sywB)

        for ec in range(8):
            def mmv(mm, ec=ec):
                B.w(mm, "ve", "sc", "ld")
                for t in range(8):
                    ins = nc.tensor.matmul(PS["psA"][:, :],
                                           S["wbuf"][:, 2048 * t + 128 * ec:2048 * t + 128 * (ec + 1)],
                                           S[f"lnc{t}"][:, :], start=(t == 0), stop=(t == 7))
                B.i(ins, "mm")
            blk.tensor(mmv)

            def scv(ve, ec=ec):
                B.w(ve, "mm")
                B.i(ve.tensor_scalar(S["qkv"][:, 16 + ec, :], PS["psA"][:, :],
                                     S["binS"][:, 16 + ec:17 + ec], None, ALU.add), "ve")
            blk.vector(scv)

        for h in range(NH):
            r0 = 64 * (h % 2)
            qh = S["qkv"][r0:r0 + 64, h // 2, :]
            kh = S["qkv"][r0:r0 + 64, 8 + h // 2, :]
            vh = S["qkv"][r0:r0 + 64, 16 + h // 2, :]

            sbank = ["psC", "psD", "psE", "psF"]

            def mms(mm, kh=kh, qh=qh, vh=vh, r0=r0):
                B.w(mm, "sc", "ve")
                for kt in range(4):
                    nc.tensor.matmul(PS[sbank[kt]][:, 128 * kt:512],
                                     kh[:, 128 * kt:128 * (kt + 1)], qh[:, 128 * kt:512],
                                     start=True, stop=True)
                for kt in range(4):
                    ins = nc.tensor.transpose(PS["ptrB"][:, 64 * kt:64 * (kt + 1)],
                                              vh[:, 128 * kt:128 * (kt + 1)],
                                              S["identB"][r0:r0 + 64, r0:r0 + 64])
                B.i(ins, "mm")
            blk.tensor(mms)

            def scs(sc):
                B.w(sc, "mm")
                sc.activation(out=S["tb16b"][:, 0:512], in_=PS["psC"][:, 0:512],
                              func=AF.Exp, scale=0.125)
                sc.activation(out=S["tb16b"][:, 512:896], in_=PS["psD"][:, 128:512],
                              func=AF.Exp, scale=0.125)
                sc.activation(out=S["tb16"][:, 0:256], in_=PS["psE"][:, 256:512],
                              func=AF.Exp, scale=0.125)
                B.i(sc.activation(out=S["tb16"][:, 256:384], in_=PS["psF"][:, 384:512],
                                  func=AF.Exp, scale=0.125), "sc")
            blk.scalar(scs)

            def ves(ve):
                B.w(ve, "sc", "mm")
                ve.tensor_tensor(S["pt"][:, 0, :], S["tb16b"][:, 0:512],
                                 S["cmask"][:, 0, :], ALU.mult)
                ve.tensor_tensor(S["pt"][:, 1, 128:512], S["tb16b"][:, 512:896],
                                 S["cmask"][:, 1, 128:512], ALU.mult)
                ve.tensor_tensor(S["pt"][:, 2, 256:512], S["tb16"][:, 0:256],
                                 S["cmask"][:, 2, 256:512], ALU.mult)
                ve.tensor_tensor(S["pt"][:, 3, 384:512], S["tb16"][:, 256:384],
                                 S["cmask"][:, 3, 384:512], ALU.mult)
                for kt in range(4):
                    ins = ve.tensor_copy(S["vt"][:, kt, 0:64],
                                         PS["ptrB"][:, 64 * kt:64 * (kt + 1)])
                B.i(ins, "ve")
            blk.vector(ves)

            def mmpv(mm):
                B.w(mm, "ve", "sc")
                for qt in range(4):
                    for kt in range(qt + 1):
                        ins = nc.tensor.matmul(PS["psA"][:, 65 * qt:65 * qt + 65],
                                               S["pt"][:, kt, 128 * qt:128 * (qt + 1)],
                                               S["vt"][:, kt, :], start=(kt == 0),
                                               stop=(kt == qt))
                B.i(ins, "mm")
            blk.tensor(mmpv)

            def vepv(ve):
                B.w(ve, "mm")
                for qt in range(4):
                    ve.reciprocal(S["t32a"][:, qt:qt + 1], PS["psA"][:, 65 * qt + 64:65 * qt + 65])
                for qt in range(4):
                    ve.tensor_copy(S["otm"][:, qt, :], PS["psA"][:, 65 * qt:65 * qt + 64])
                for qt in range(4):
                    ins = ve.tensor_scalar(S["otm"][:, qt, :], S["otm"][:, qt, :],
                                           S["t32a"][:, qt:qt + 1], None, ALU.mult)
                B.i(ins, "ve")
            blk.vector(vepv)

            def mmo(mm):
                B.w(mm, "ve")
                for qt in range(4):
                    ins = nc.tensor.transpose(PS["ptrB"][0:64, 128 * qt:128 * (qt + 1)],
                                              S["otm"][:, qt, :], S["identB"][:, :])
                B.i(ins, "mm")
            blk.tensor(mmo)

            def veo(ve, h=h, r0=r0):
                B.w(ve, "mm")
                for qt in range(4):
                    ins = ve.tensor_copy(S["ofm"][r0:r0 + 64, h // 2, 128 * qt:128 * (qt + 1)],
                                         PS["ptrB"][0:64, 128 * qt:128 * (qt + 1)])
                B.i(ins, "ve")
            blk.vector(veo)

        for dt in range(8):
            def mmop(mm, dt=dt):
                B.w(mm, "ve", "sc")
                for ct in range(8):
                    ins = nc.tensor.matmul(PS["psF"][:, :],
                                           S["wbuf"][:, 2048 * ct + 1024 + 128 * dt:2048 * ct + 1024 + 128 * (dt + 1)],
                                           S["ofm"][:, ct, :], start=(ct == 0), stop=(ct == 7))
                B.i(ins, "mm")
            blk.tensor(mmop)

            def scop(ve, dt=dt):
                B.w(ve, "mm")
                B.i(ve.tensor_scalar(S["tb16b"][:, 0:512], PS["psF"][:, :],
                                     S["boutS"][:, dt:dt + 1], None, ALU.add), "ve")
            blk.vector(scop)

            def veop(ve, dt=dt, seg=seg):
                B.w(ve, "sc")
                dst = S[f"res{dt}"][:, :].rearrange("p (a s r) -> p a s r", a=2, r=4)[:, seg, :, 0]
                B.i(ve.tensor_tensor(dst, dst, S["tb16b"][:, 0:512], ALU.add), "ve")
            blk.vector(veop)


def _p_output(ctx):
    nc, B, blk, S, DR = (ctx[k] for k in ["nc", "B", "blk", "S", "DR"])
    out_w, out_t = ctx["out_w"], ctx["out_t"]
    fmfull = (DR["bounce"].ap().rearrange("tk d -> (tk d)")
              .rearrange("(d s) -> d s", d=D))

    def sq1(sc):
        B.w(sc, "ve", "ld")
        for t in range(8):
            B.i(sc.dma_start(out=fmfull[128 * t:128 * (t + 1), :], in_=S[f"res{t}"][:, :]),
                "sq", 16)
    blk.scalar(sq1)

    bufs = [S["eo"], S["xs"], S["tb16"], S["tb16b"]]
    ld_after = {}

    def emit_tr(w):
        def sy1(sy, w=w):
            B.w(sy, "sq", "gq")
            B.i(sy.dma_start_transpose(bufs[w % 4][:, :], fmfull[:, 128 * w:128 * (w + 1)]),
                "ld", 16)
        blk.sync(sy1)
        ld_after[w] = B.v["ld"]

    def emit_wr(w):
        def gq1(gp, w=w):
            gp.wait_ge(B.h["ld"], ld_after[w])
            dst = (out_w if w < 16 else out_t)
            ww = w % 16
            B.i(gp.dma_start(out=dst[128 * ww:128 * (ww + 1), :], in_=bufs[w % 4][:, :]),
                "gq", 16)
        blk.gpsimd(gq1)

    for w in range(4):
        emit_tr(w)
    for w in range(32):
        emit_wr(w)
        if w + 4 < 32:
            emit_tr(w + 4)

    def fin(gp):
        B.w(gp, "gq", "ld", "sq")
        gp.nop()
    blk.gpsimd(fin)


def kernel(**inputs):
    maps = _host_prep(inputs)
    nc = _build()
    res = run_bass_kernel_spmd(nc, maps, core_ids=list(range(NCORE)))
    out = np.zeros((8, 2, SEG, D), np.float32)
    for c in range(NCORE):
        out[c // 2, c % 2] = res.results[c]["out_w"]
        out[c // 2 + 4, c % 2] = res.results[c]["out_t"]
    return np.ascontiguousarray(out.reshape(8, 4096, D)[:, :, PD])


# version 13
# speedup vs baseline: 1.0326x; 1.0010x over previous
import contextlib
import numpy as np
import ml_dtypes
import concourse.bass as bass
import concourse.mybir as mybir
from concourse.bass_utils import run_bass_kernel_spmd
from concourse.bass import IndirectOffsetOnAxis

F32 = mybir.dt.float32
BF16 = mybir.dt.bfloat16
F16 = mybir.dt.float16
I32 = mybir.dt.int32
U16 = mybir.dt.uint16
AF = mybir.ActivationFunctionType
ALU = mybir.AluOpType
AX = mybir.AxisListType
bf = ml_dtypes.bfloat16

D = 1024
HID = 768
NE = 64
CAP = 256
NCORE = 8
SEG = 2048
TOK = 2 * SEG
WIN = 16384
EPC = NE // NCORE
NSLOT = EPC * CAP
NH = 16
DH = 64
M = 512
TBL_ROWS = WIN + 128
TRASH_ROW = WIN + 1
EO_PAD = 16
EO_ROWS = NSLOT + EO_PAD
GUW = 2 * HID
WROW = 8 * GUW + 6 * D
STAGE = 99


def _p4g(n):
    d = np.arange(n)
    return (d // 128) * 128 + (d % 128 % 4) * 32 + (d % 128) // 4


def _inv(p):
    q = np.empty_like(p)
    q[p] = np.arange(len(p))
    return q


PD = _p4g(D)
PI = _p4g(HID)
INV_PD = _inv(PD)
INV_PI = _inv(PI)


def _pack_u16(packed_i32):
    b = np.ascontiguousarray(packed_i32.astype(np.int32)).view(np.uint8)
    b = np.ascontiguousarray(b.reshape(*packed_i32.shape, 4)[..., 0])
    return b.view(np.uint16)


def _host_prep(inputs):
    x = inputs["L0_attn_x"]
    wqkvT = np.ascontiguousarray(inputs["L0_attn_w_in"].T[INV_PD, :], np.float32)
    woutT = np.ascontiguousarray(inputs["L0_attn_w_out"].T[:, INV_PD], np.float32)
    gateT = np.ascontiguousarray(
        inputs["L0_ffn_gate_w"].T[INV_PD, :].astype(np.float32).reshape(8, 128, NE)
        .transpose(1, 0, 2))
    b_out_p = np.ascontiguousarray(
        inputs["L0_attn_b_out"][INV_PD].astype(np.float32).reshape(8, 128).T)
    b_in_c = np.ascontiguousarray(
        inputs["L0_attn_b_in"].astype(np.float32).reshape(24, 128).T)

    gp = _pack_u16(inputs["L0_ffn_gate_packed"])[:, INV_PI, :]
    up = _pack_u16(inputs["L0_ffn_up_packed"])[:, INV_PI, :]
    gup = np.concatenate([gp, up], axis=1)
    gup_sc = np.concatenate(
        [inputs["L0_ffn_gate_scales"][:, INV_PI, :],
         inputs["L0_ffn_up_scales"][:, INV_PI, :]], axis=1).astype(np.float32)
    dn = _pack_u16(inputs["L0_ffn_down_packed"])[:, INV_PD, :]
    dn_sc = inputs["L0_ffn_down_scales"][:, INV_PD, :].astype(np.float32)

    shgup = np.concatenate([_pack_u16(inputs["L0_ffn_sh_gate_packed"])[INV_PI, :],
                            _pack_u16(inputs["L0_ffn_sh_up_packed"])[INV_PI, :]], axis=0)
    shgup_sc = np.concatenate(
        [inputs["L0_ffn_sh_gate_scales"][INV_PI, :],
         inputs["L0_ffn_sh_up_scales"][INV_PI, :]], axis=0).astype(np.float32)
    shdn = _pack_u16(inputs["L0_ffn_sh_down_packed"])[INV_PD, :]
    shdn_sc = inputs["L0_ffn_sh_down_scales"][INV_PD, :].astype(np.float32)

    p128 = np.arange(128, dtype=np.int32)
    xr = x.reshape(8, 2, SEG, D)
    maps = []
    for c in range(NCORE):
        maps.append({
            "xw": np.ascontiguousarray(xr[c // 2, c % 2], np.float32),
            "xt": np.ascontiguousarray(xr[c // 2 + 4, c % 2], np.float32),
            "wqkvT": wqkvT, "woutT": woutT, "gateT": gateT,
            "b_out_p": b_out_p, "b_in_c": b_in_c,
            "gup_pk": np.ascontiguousarray(gup[EPC * c:EPC * (c + 1)]),
            "gup_sc": np.ascontiguousarray(gup_sc[EPC * c:EPC * (c + 1)]),
            "dn_pk": np.ascontiguousarray(dn[EPC * c:EPC * (c + 1)]),
            "dn_sc": np.ascontiguousarray(dn_sc[EPC * c:EPC * (c + 1)]),
            "sh_gup_pk": shgup, "sh_gup_sc": shgup_sc,
            "sh_dn_pk": shdn, "sh_dn_sc": shdn_sc,
            "tokbase": (p128.astype(np.float32) + 2048.0 * c).reshape(128, 1),
            "slrow": (p128 * 128 + 16 * c).astype(np.int32).reshape(128, 1),
            "maskpre": np.repeat((np.arange(8) < c).astype(np.float32).reshape(1, 8),
                                 64, axis=0),
        })
    return maps


class _B:
    def __init__(self, nc, es):
        self.nc = nc
        self.v = {}
        self.h = {}
        for s in ["ld", "gq", "sq", "cc", "mm", "ve", "sc", "gp"]:
            self.h[s] = es.enter_context(nc.semaphore(s))
            self.v[s] = 0

    def w(self, eng, *sems):
        for s in sems:
            if self.v[s] > 0:
                eng.wait_ge(self.h[s], self.v[s])

    def i(self, ins, sem, k=1):
        ins.then_inc(self.h[sem], k)
        self.v[sem] += k
        return ins


def _build(stage=None, debug=False, ffnsub=99):
    if stage is None:
        stage = STAGE
    nc = bass.Bass("TRN2", target_bir_lowering=False)
    es = contextlib.ExitStack()
    B = _B(nc, es)

    A = {}
    for name, shp, dt in [
        ("xw", [SEG, D], F32), ("xt", [SEG, D], F32),
        ("wqkvT", [D, 3 * D], F32), ("woutT", [D, D], F32), ("gateT", [128, 8, NE], F32),
        ("b_out_p", [128, 8], F32), ("b_in_c", [128, 24], F32),
        ("gup_pk", [EPC, GUW, 256], U16), ("gup_sc", [EPC, GUW, 8], F32),
        ("dn_pk", [EPC, D, 192], U16), ("dn_sc", [EPC, D, 6], F32),
        ("sh_gup_pk", [GUW, 256], U16), ("sh_gup_sc", [GUW, 8], F32),
        ("sh_dn_pk", [D, 192], U16), ("sh_dn_sc", [D, 6], F32),
        ("tokbase", [128, 1], F32), ("slrow", [128, 1], I32),
        ("maskpre", [64, 8], F32),
    ]:
        A[name] = nc.dram_tensor(name, shp, dt, kind="ExternalInput").ap()
    out_w = nc.dram_tensor("out_w", [SEG, D], F32, kind="ExternalOutput").ap()
    out_t = nc.dram_tensor("out_t", [SEG, D], F32, kind="ExternalOutput").ap()
    dbg = nc.dram_tensor("dbg", [128, 4096], F32, kind="ExternalOutput").ap() if debug else None

    DR = {}
    for name, shp, dt, kw in [
        ("wstage", [EPC, 128, WROW], BF16, {}),
        ("shstage", [128, WROW], BF16, {}),
        ("attnw", [128, 2 * 16384], BF16, {}),
        ("bounce", [TOK, D], BF16, {}),
        ("lnw_in", [SEG + EO_PAD, D], BF16, {}),

        ("cnt_in", [64, 1], F32, {}),

        ("tbl_in", [TBL_ROWS, 2], F32, {}),
        ("tblsum", [WIN, 2], F32, {}),
        ("eo_in", [EO_ROWS, D], BF16, {}),
        ("sharena", [NCORE * (SEG + EO_PAD) * D // 2 + NCORE * 64
                     + NCORE * EO_ROWS * D // 2 + NCORE * TBL_ROWS * 2], F32,
         {"addr_space": "Shared"}),
    ]:
        DR[name] = nc.dram_tensor(name, shp, dt, **kw)
    _sa = DR["sharena"].ap()
    _o1 = NCORE * (SEG + EO_PAD) * D // 2
    _o2 = _o1 + NCORE * 64
    _o3 = _o2 + NCORE * EO_ROWS * D // 2
    VIEWS = {
        "lnw_out": _sa[0:_o1].bitcast(BF16).rearrange("(r d) -> r d", d=D),
        "cnt_out": _sa[_o1:_o2].rearrange("(r c) -> r c", c=1),
        "eo_out": _sa[_o2:_o3].bitcast(BF16).rearrange("(r d) -> r d", d=D),
        "gall": DR["sharena"].ap().bitcast(BF16).rearrange("(r d) -> r d", d=D),
        "eo_row0": _o2 * 2 // D,
        "tbl_ag": _sa[_o3:_o3 + NCORE * TBL_ROWS * 2],
        "tbl_rk": (lambda r, _sa=_sa, _o3=_o3:
                   _sa[_o3 + r * TBL_ROWS * 2:_o3 + r * TBL_ROWS * 2 + WIN * 2]
                   .rearrange("(p a) -> p a", p=128)),
    }

    S = {}
    def tile(name, shape, dt):
        S[name] = es.enter_context(nc.sbuf_tensor(name, shape, dt))

    for t in range(8):
        tile(f"res{t}", [128, TOK], BF16)
    for t in range(8):
        tile(f"lnc{t}", [128, 512], BF16)
    tile("ones", [128, 128], BF16)
    tile("identB", [128, 128], BF16)
    tile("identF", [128, 128], F32)
    tile("iotaE", [128, 64], F32)
    tile("cmask", [128, 4, M], BF16)
    tile("gateTs", [128, 8, NE], BF16)
    tile("binS", [128, 24], F32)
    tile("boutS", [128, 8], F32)
    tile("slrows", [128, 1], I32)
    tile("tokb", [128, 1], F32)
    tile("trashp", [128, 1], F32)
    tile("eps1", [128, 1], F32)
    tile("maskp", [64, 8], F32)
    tile("wbuf", [128, WROW], BF16)
    tile("pk", [128, 256], U16)
    tile("pksc", [128, 8], F32)
    tile("bits", [128, 1024], U16)
    tile("bita", [128, 256], U16)
    tile("bitb", [128, 256], U16)
    tile("wnat", [128, 1024], BF16)
    tile("h6", [128, 6, 512], BF16)
    tile("gb", [128, 2, D], BF16)
    tile("gbT", [128, 8, 256], BF16)
    tile("eo", [128, D], BF16)
    tile("tslice", [128, 32], F32)
    tile("tbb", [128, 2, 256], F32)
    tile("tidx", [128, 16], I32)
    tile("twgt", [128, 16], F32)
    tile("logit", [128, 16, NE], F32)
    tile("mx8", [128, 16, 8], F32)
    tile("wj4", [128, 16, 4], F32)
    tile("amask", [128, 64], BF16)
    tile("scanb", [64, SEG], F32)
    tile("cnt8", [64, 8], F32)
    tile("base", [64, 1], F32)
    tile("post", [128, 64], F32)
    tile("sig", [128, 16, 4], I32)
    tile("scat4", [128, 4, 2], F32)
    tile("i32a", [128, 64], I32)
    tile("i32b", [128, 64], I32)
    tile("i32c", [128, 64], I32)
    tile("qkv", [128, 24, M], BF16)
    tile("pt", [128, 4, M], BF16)
    tile("vt", [128, 4, DH + 1], BF16)
    tile("otm", [128, 4, DH], BF16)
    tile("ofm", [128, 8, M], BF16)
    tile("xs", [128, D], BF16)
    tile("t32a", [128, 512], F32)
    tile("t32b", [128, 512], F32)
    tile("t32c", [128, 512], F32)
    tile("t32d", [128, 512], F32)
    tile("tb16", [128, 1024], BF16)
    tile("tb16b", [128, 1024], BF16)
    tile("f32x", [128, 1024], F32)

    PS = {}
    def ptile(name, shape, dt):
        PS[name] = es.enter_context(nc.psum_tensor(name, shape, dt))
    for n in ["psA", "psB", "psC", "psD", "psE", "psF"]:
        ptile(n, [128, 512], F32)
    ptile("ptrA", [128, 1024], BF16)
    ptile("ptrB", [128, 1024], BF16)

    blk = es.enter_context(nc.Block())
    ctx = dict(nc=nc, B=B, blk=blk, A=A, DR=DR, S=S, PS=PS, V=VIEWS,
               out_w=out_w, out_t=out_t, dbg=dbg, stage=stage, ffnsub=ffnsub)
    _emit_all(ctx)
    es.close()
    return nc



def _emit_all(ctx):
    _p_consts(ctx)
    _p_xload(ctx)
    wrest = None
    if ctx["stage"] >= 1:
        wunits = _p_weights_units(ctx)
        wunits[0]()
        wrest = wunits[1:]
    if ctx["stage"] >= 2:
        _p_ffn(ctx, 0, interleave=wrest)
    elif wrest:
        for f in wrest:
            f()
    if ctx["stage"] >= 3:
        _p_attn(ctx)
    if ctx["stage"] >= 4:
        _p_ffn(ctx, 1)
    _p_output(ctx)


def _p_consts(ctx):
    nc, B, blk, A, S, DR = (ctx[k] for k in ["nc", "B", "blk", "A", "S", "DR"])

    def gp_fn(gp):
        gp.iota(S["iotaE"][:, :], [[1, 64]], base=0, channel_multiplier=0,
                allow_small_or_imprecise_dtypes=True)
        gp.memset(S["ones"][:, :], 1.0)
        gp.memset(S["vt"][:, :, :], 1.0)
        gp.memset(S["xs"][:, :], 0.0)
        gp.memset(S["eps1"][:, :], 1e-5)
        for kt, t in enumerate(["t32a", "t32b", "t32c", "t32d"]):
            gp.iota(S[t][:, :], [[1, M]], base=-128 * kt, channel_multiplier=-1,
                    allow_small_or_imprecise_dtypes=True)
        gp.iota(S["trashp"][:, :], [[1, 1]], base=WIN, channel_multiplier=1,
                allow_small_or_imprecise_dtypes=True)
        B.i(gp.iota(S["f32x"][:, 0:128], [[1, 128]], base=0, channel_multiplier=-1,
                    allow_small_or_imprecise_dtypes=True), "gp")
    blk.gpsimd(gp_fn)

    def sy_fn(sy):
        B.i(sy.dma_start(out=S["f32x"][:, 512:1024].rearrange("p (a e) -> p a e", e=NE),
                         in_=A["gateT"]), "ld", 16)
        B.i(sy.dma_start(out=S["binS"][:, :], in_=A["b_in_c"]), "ld", 16)
        B.i(sy.dma_start(out=S["boutS"][:, :], in_=A["b_out_p"]), "ld", 16)
        B.i(sy.dma_start(out=S["maskp"][:, :], in_=A["maskpre"]), "ld", 16)
        B.i(sy.dma_start(out=S["tokb"][:, :], in_=A["tokbase"]), "ld", 16)
        B.i(sy.dma_start(out=S["slrows"][:, :], in_=A["slrow"]), "ld", 16)
    blk.sync(sy_fn)

    def ve_fn(ve):
        B.w(ve, "gp", "ld")
        ve.tensor_scalar(S["identF"][:, :], S["f32x"][:, 0:128], 0.0, None, ALU.is_equal)
        for t in ["t32a", "t32b", "t32c", "t32d"]:
            ve.tensor_scalar(S[t][:, :], S[t][:, :], 0.0, None, ALU.is_ge)
        ve.tensor_copy(S["identB"][:, :], S["identF"][:, :])
        for kt, t in enumerate(["t32a", "t32b", "t32c", "t32d"]):
            ve.tensor_copy(S["cmask"][:, kt, :], S[t][:, :])
        B.i(ve.tensor_copy(S["gateTs"][:, :, :],
                           S["f32x"][:, 512:1024].rearrange("p (a e) -> p a e", e=NE)), "ve")
    blk.vector(ve_fn)

    def gq_fn(gp):
        B.w(gp, "gp")
        B.i(gp.dma_start(out=DR["lnw_in"].ap()[SEG:SEG + EO_PAD, :],
                         in_=S["xs"][0:EO_PAD, 0:D]), "gq", 16)
        B.i(gp.dma_start(out=DR["eo_in"].ap()[NSLOT:EO_ROWS, :],
                         in_=S["xs"][0:EO_PAD, 0:D]), "gq", 16)
    blk.gpsimd(gq_fn)


def _p_xload(ctx):
    nc, B, blk, A, S, DR = (ctx[k] for k in ["nc", "B", "blk", "A", "S", "DR"])
    xin = [A["xw"], A["xt"]]
    for s in range(2):
        for ch in range(16):
            first = (s == 0 and ch == 0)

            def sy(sy, s=s, ch=ch, first=first):
                B.w(sy, "ve")
                B.i(sy.dma_start(out=S["f32x"][:, :],
                                 in_=xin[s][128 * ch:128 * (ch + 1), :]), "ld", 16)
            blk.sync(sy)

            def ve(ve, first=first):
                B.w(ve, "ld")
                if not first:
                    B.w(ve, "sq")
                src = S["f32x"][:, :].rearrange("p (g q f) -> p g q f", g=8, q=32)
                dst = S["tb16"][:, :].rearrange("p (g k q) -> p g k q", g=8, k=4)
                for k in range(4):
                    ins = ve.tensor_copy(dst[:, :, k, :], src[:, :, :, k])
                B.i(ins, "ve")
            blk.vector(ve)

            def sq(sc, s=s, ch=ch):
                B.w(sc, "ve")
                B.i(sc.dma_start(
                    out=DR["bounce"].ap()[SEG * s + 128 * ch:SEG * s + 128 * (ch + 1), :],
                    in_=S["tb16"][:, 0:D]), "sq", 16)
            blk.scalar(sq)

    def sy2(sy):
        B.w(sy, "sq")
        for t in range(8):
            B.i(sy.dma_start_transpose(S[f"res{t}"][:, :],
                                       DR["bounce"].ap()[:, 128 * t:128 * (t + 1)]), "ld", 16)
    blk.sync(sy2)


def _dequant_rowtile(ctx, pk_src, sc_src, ngrp):
    nc, B, blk, S = (ctx[k] for k in ["nc", "B", "blk", "S"])
    ncol = 32 * ngrp

    def syl(sy):
        B.w(sy, "sc")
        B.i(sy.dma_start(out=S["pk"][:, 0:ncol], in_=pk_src), "ld", 16)
        B.i(sy.dma_start(out=S["pksc"][:, 0:ngrp], in_=sc_src), "ld", 16)
    blk.sync(syl)

    def vef(ve):
        B.w(ve, "ld", "sc")
        ve.tensor_scalar(S["pksc"][:, 0:ngrp], S["pksc"][:, 0:ngrp],
                         float(2 ** 14), None, ALU.mult)
        for k in range(4):
            mag_sh = 9 - 4 * k
            ve.tensor_scalar(S["bita"][:, 0:ncol], S["pk"][:, 0:ncol], 7 << (4 * k),
                             abs(mag_sh), ALU.bitwise_and,
                             ALU.logical_shift_left if mag_sh >= 0 else ALU.logical_shift_right)
            ve.tensor_scalar(S["bitb"][:, 0:ncol], S["pk"][:, 0:ncol], 8 << (4 * k),
                             12 - 4 * k, ALU.bitwise_and, ALU.logical_shift_left)
            dst = S["bits"][:, 0:128 * ngrp].rearrange("p (g k q) -> p g k q", k=4, q=32)
            ins = ve.tensor_tensor(dst[:, :, k, :],
                                   S["bita"][:, 0:ncol].rearrange("p (g q) -> p g q", q=32),
                                   S["bitb"][:, 0:ncol].rearrange("p (g q) -> p g q", q=32),
                                   ALU.bitwise_or)
        B.i(ins, "ve")
    blk.vector(vef)

    def scf(ve):
        B.w(ve, "mm")
        for g in range(ngrp):
            ins = ve.tensor_scalar(S["wnat"][:, 128 * g:128 * (g + 1)],
                                   S["bits"][:, 128 * g:128 * (g + 1)].bitcast(F16),
                                   S["pksc"][:, g:g + 1], None, ALU.mult)
        B.i(ins, "sc")
    blk.vector(scf)


def _p_weights_units(ctx):
    nc, B, blk, A, S, DR, PS = (ctx[k] for k in ["nc", "B", "blk", "A", "S", "DR", "PS"])
    units = []

    def emit_set(sname, e):
        if sname == "sh":
            pk_gu, sc_gu, pk_dn, sc_dn = (A["sh_gup_pk"], A["sh_gup_sc"],
                                          A["sh_dn_pk"], A["sh_dn_sc"])
        else:
            pk_gu, sc_gu = A["gup_pk"][e], A["gup_sc"][e]
            pk_dn, sc_dn = A["dn_pk"][e], A["dn_sc"][e]

        for kind, rt in [("gu", rt) for rt in range(12)] + [("dn", rt) for rt in range(8)]:
            ngrp = 8 if kind == "gu" else 6
            pk_src = (pk_gu if kind == "gu" else pk_dn)[128 * rt:128 * (rt + 1), :]
            sc_src = (sc_gu if kind == "gu" else sc_dn)[128 * rt:128 * (rt + 1), :]
            _dequant_rowtile(ctx, pk_src, sc_src, ngrp)

            def mmf(mm, ngrp=ngrp):
                B.w(mm, "sc", "ve")
                for cc in range(ngrp):
                    ins = nc.tensor.transpose(PS["ptrA"][:, 128 * cc:128 * (cc + 1)],
                                              S["wnat"][:, 128 * cc:128 * (cc + 1)],
                                              S["identB"][:, :])
                B.i(ins, "mm")
            blk.tensor(mmf)

            if sname == "sh":
                def vee(ve, kind=kind, rt=rt, ngrp=ngrp):
                    B.w(ve, "mm", "sq")
                    if kind == "gu":
                        dst = S["wbuf"][:, 0:8 * GUW].rearrange("p (a b) -> p a b", a=8)
                        dst = dst[:, :, 128 * rt:128 * (rt + 1)]
                        src = PS["ptrA"][:, :].rearrange("p (a q) -> p a q", a=8)
                    else:
                        dst = S["wbuf"][:, 8 * GUW:].rearrange("p (a b) -> p a b", a=6)
                        dst = dst[:, :, 128 * rt:128 * (rt + 1)]
                        src = PS["ptrA"][:, 0:768].rearrange("p (a q) -> p a q", a=6)
                    B.i(ve.tensor_copy(dst, src), "ve")
                blk.vector(vee)
            else:
                def vee(ve, ngrp=ngrp):
                    B.w(ve, "mm", "sq")
                    B.i(ve.tensor_copy(S["tb16"][:, 0:128 * ngrp],
                                       PS["ptrA"][:, 0:128 * ngrp]), "ve")
                blk.vector(vee)

                def sqs(sc, kind=kind, rt=rt, ngrp=ngrp, e=e):
                    B.w(sc, "ve")
                    wst = DR["wstage"].ap()[e]
                    if kind == "gu":
                        dst = wst[:, 0:8 * GUW].rearrange("p (a b) -> p a b", a=8)
                    else:
                        dst = wst[:, 8 * GUW:].rearrange("p (a b) -> p a b", a=6)
                    dst = dst[:, :, 128 * rt:128 * (rt + 1)]
                    B.i(sc.dma_start(out=dst,
                                     in_=S["tb16"][:, 0:128 * ngrp].rearrange(
                                         "p (a b) -> p a b", b=128)), "sq", 16)
                blk.scalar(sqs)

        if sname == "sh":
            def sys(sc):
                B.w(sc, "ve")
                B.i(sc.dma_start(out=DR["shstage"].ap(), in_=S["wbuf"][:, :]), "sq", 16)
            blk.scalar(sys)

    units.append(lambda: emit_set("sh", None))
    for e in range(EPC):
        units.append(lambda e=e: emit_set("ex", e))

    def _cast_store(pairs):
        for sap, dap in pairs:
            def syj(sy, sap=sap):
                B.w(sy, "ve")
                B.i(sy.dma_start(out=S["f32x"][:, :], in_=sap), "ld", 16)
            blk.sync(syj)

            def vej(ve):
                B.w(ve, "ld", "sq")
                B.i(ve.tensor_copy(S["tb16"][:, 0:1024], S["f32x"][:, :]), "ve")
            blk.vector(vej)

            def sqj(sc, dap=dap):
                B.w(sc, "ve")
                B.i(sc.dma_start(out=dap, in_=S["tb16"][:, 0:1024]), "sq", 16)
            blk.scalar(sqj)

    pairs = []
    for a in range(8):
        for u in range(2):
            pairs.append((A["wqkvT"][128 * a:128 * (a + 1), 1024 * u:1024 * (u + 1)],
                          DR["attnw"].ap()[:, 2048 * a + 1024 * u:2048 * a + 1024 * (u + 1)]))
    for a in range(8):
        pairs.append((A["wqkvT"][128 * a:128 * (a + 1), 2048:3072],
                      DR["attnw"].ap()[:, 16384 + 2048 * a:16384 + 2048 * a + 1024]))
    for a in range(8):
        pairs.append((A["woutT"][128 * a:128 * (a + 1), :],
                      DR["attnw"].ap()[:, 16384 + 2048 * a + 1024:16384 + 2048 * (a + 1)]))
    units.append(lambda: _cast_store(pairs[0:16]))
    units.append(lambda: _cast_store(pairs[16:32]))
    return units


def _ln_chunk(ctx, ch, src, dstpref="lnc", width=512):
    nc, B, blk, S, PS = (ctx[k] for k in ["nc", "B", "blk", "S", "PS"])
    cols = slice(512 * ch, 512 * ch + width)

    def mm1(mm):
        B.w(mm, "ve", "sc")
        for t in range(8):
            ins = nc.tensor.matmul(PS["psD"][:, 0:width], S["ones"][:, :],
                                   S[src + str(t)][:, cols], start=(t == 0), stop=(t == 7))
        B.i(ins, "mm")
    blk.tensor(mm1)

    for t in range(8):
        def scx(ve, t=t):
            B.w(ve, "mm")
            B.i(ve.tensor_tensor(S["tb16b"][:, 0:width], S[src + str(t)][:, cols],
                                 S[src + str(t)][:, cols], ALU.mult), "sc")
        blk.vector(scx)

        def mm2(mm, t=t):
            B.w(mm, "sc")
            B.i(nc.tensor.matmul(PS["psE"][:, 0:width], S["ones"][:, :],
                                 S["tb16b"][:, 0:width], start=(t == 0), stop=(t == 7)), "mm")
        blk.tensor(mm2)

    def ve1(ve):
        B.w(ve, "mm")
        ve.tensor_scalar(S["t32a"][:, 0:width], PS["psD"][:, 0:width], 1.0 / D, None, ALU.mult)
        ve.tensor_scalar(S["t32b"][:, 0:width], PS["psE"][:, 0:width], 1.0 / D, None, ALU.mult)
        ve.tensor_tensor(S["t32c"][:, 0:width], S["t32a"][:, 0:width], S["t32a"][:, 0:width],
                         ALU.mult)
        ve.tensor_tensor(S["t32b"][:, 0:width], S["t32b"][:, 0:width],
                         S["t32c"][:, 0:width], ALU.subtract)
        ve.tensor_scalar(S["t32b"][:, 0:width], S["t32b"][:, 0:width], 1e-5,
                         None, ALU.add)
        B.i(ve.reciprocal(S["t32c"][:, 0:width], S["t32b"][:, 0:width]), "ve")
    blk.vector(ve1)

    def sc1(sc):
        B.w(sc, "ve")
        B.i(sc.activation(out=S["t32d"][:, 0:width], in_=S["t32c"][:, 0:width],
                          func=AF.Sqrt), "sc")
    blk.scalar(sc1)

    def ve2(ve):
        B.w(ve, "sc", "mm", "sq")
        for t in range(8):
            ve.tensor_tensor(S["tb16b"][:, 0:width], S[src + str(t)][:, cols],
                             S["t32a"][:, 0:width], ALU.subtract)
            ins = ve.tensor_tensor(S[dstpref + str(t)][:, 0:width], S["tb16b"][:, 0:width],
                                   S["t32d"][:, 0:width], ALU.mult)
        B.i(ins, "ve")
    blk.vector(ve2)


def _fmv(ctx):
    return (ctx["DR"]["bounce"].ap().rearrange("tk d -> (tk d)")[0:D * SEG]
            .rearrange("(d s) -> d s", d=D))


def _stage_and_route(ctx, call, ch):
    nc, B, blk, A, S, DR, PS = (ctx[k] for k in ["nc", "B", "blk", "A", "S", "DR", "PS"])
    fmv = _fmv(ctx)

    def sq2(sc):
        B.w(sc, "ve", "ld")
        for t in range(8):
            B.i(sc.dma_start(out=fmv[128 * t:128 * (t + 1), 512 * ch:512 * (ch + 1)],
                             in_=S[f"lnc{t}"][:, :]), "sq", 16)
    blk.scalar(sq2)

    for sub in range(4):
        wc = 4 * ch + sub
        tcols = slice(128 * sub, 128 * (sub + 1))

        def mmr(mm, tcols=tcols):
            B.w(mm, "ve", "sc")
            for t in range(8):
                ins = nc.tensor.matmul(PS["psF"][:, 0:NE], S[f"lnc{t}"][:, tcols],
                                       S["gateTs"][:, t, :], start=(t == 0), stop=(t == 7))
            B.i(ins, "mm")
        blk.tensor(mmr)

        def ver(ve, wc=wc):
            B.w(ve, "mm")
            ve.tensor_copy(S["logit"][:, wc, :], PS["psF"][:, 0:NE])
            ve.max(S["mx8"][:, wc, :], S["logit"][:, wc, :])
            B.i(ve.tensor_scalar(S["t32a"][:, 0:1], S["mx8"][:, wc, 0:1], -1.0, None,
                                 ALU.mult), "ve")
        blk.vector(ver)

        def scr(sc, wc=wc):
            B.w(sc, "ve")
            B.i(sc.activation(out=S["wj4"][:, wc, :], in_=S["mx8"][:, wc, 0:4], func=AF.Exp,
                              bias=S["t32a"][:, 0:1], scale=1.0), "sc")
        blk.scalar(scr)

        def ver2(ve, wc=wc):
            B.w(ve, "sc")
            ve.reduce_sum(S["t32b"][:, 0:1], S["wj4"][:, wc, :], axis=AX.X)
            ve.tensor_scalar(S["amask"][:, :], S["logit"][:, wc, :], S["mx8"][:, wc, 3:4],
                             None, ALU.is_ge)
            ve.reciprocal(S["t32c"][:, 0:1], S["t32b"][:, 0:1])
            ve.tensor_copy(S["t32a"][:, 0:64], S["logit"][:, wc, :])
            B.i(ve.tensor_scalar(S["wj4"][:, wc, :], S["wj4"][:, wc, :], S["t32c"][:, 0:1],
                                 None, ALU.mult), "ve")
        blk.vector(ver2)

        def mmt(mm):
            B.w(mm, "ve")
            B.i(nc.tensor.transpose(PS["ptrB"][0:64, 0:128], S["amask"][:, :],
                                    S["identB"][:, :]), "mm")
        blk.tensor(mmt)

        def vet(ve, wc=wc):
            B.w(ve, "mm")
            B.i(ve.tensor_copy(S["scanb"][0:64, 128 * wc:128 * (wc + 1)],
                               PS["ptrB"][0:64, 0:128]), "ve")
        blk.vector(vet)


def _shared_chunk(ctx, ch):
    nc, B, blk, S, PS = (ctx[k] for k in ["nc", "B", "blk", "S", "PS"])
    cols = slice(512 * ch, 512 * (ch + 1))

    for ip in range(6):
        def mmg(mm, ip=ip):
            B.w(mm, "ve", "sc")
            for t in range(8):
                nc.tensor.matmul(PS["psA"][:, :], S["wbuf"][:, GUW * t + 128 * ip:GUW * t + 128 * (ip + 1)],
                                 S[f"lnc{t}"][:, :], start=(t == 0), stop=(t == 7))
            for t in range(8):
                ins = nc.tensor.matmul(PS["psB"][:, :],
                                       S["wbuf"][:, GUW * t + 128 * (ip + 6):GUW * t + 128 * (ip + 7)],
                                       S[f"lnc{t}"][:, :], start=(t == 0), stop=(t == 7))
            B.i(ins, "mm")
        blk.tensor(mmg)

        def scg(sc):
            B.w(sc, "mm")
            B.i(sc.activation(out=S["tb16b"][:, 0:512], in_=PS["psA"][:, :], func=AF.Silu), "sc")
        blk.scalar(scg)

        def veg(ve, ip=ip):
            B.w(ve, "sc")
            B.i(ve.tensor_tensor(S["h6"][:, ip, :], S["tb16b"][:, 0:512], PS["psB"][:, :],
                                 ALU.mult), "ve")
        blk.vector(veg)

    for dt in range(8):
        def mmd(mm, dt=dt):
            B.w(mm, "ve", "sc")
            for ip in range(6):
                ins = nc.tensor.matmul(PS["psC"][:, :],
                                       S["wbuf"][:, 8 * GUW + D * ip + 128 * dt:8 * GUW + D * ip + 128 * (dt + 1)],
                                       S["h6"][:, ip, :], start=(ip == 0), stop=(ip == 5))
            B.i(ins, "mm")
        blk.tensor(mmd)

        def ved(ve, dt=dt):
            B.w(ve, "mm")
            B.i(ve.scalar_tensor_tensor(S[f"res{dt}"][:, cols], PS["psC"][:, :], 0.5,
                                        S[f"res{dt}"][:, cols], ALU.mult, ALU.add), "ve")
        blk.vector(ved)


def _stage_transpose_ag(ctx, call):
    nc, B, blk, S, DR = (ctx[k] for k in ["nc", "B", "blk", "S", "DR"])
    fmv = _fmv(ctx)
    for w in range(16):
        def sy1(sy, w=w):
            B.w(sy, "sq", "gq")
            B.i(sy.dma_start_transpose(S["eo"][:, :], fmv[:, 128 * w:128 * (w + 1)]), "ld", 16)
        blk.sync(sy1)

        def gq1(gp, w=w):
            B.w(gp, "ld", "cc")
            B.i(gp.dma_start(out=DR["lnw_in"].ap()[128 * w:128 * (w + 1), :],
                             in_=S["eo"][:, :]), "gq", 16)
        blk.gpsimd(gq1)

    def gcc(gp):
        B.w(gp, "gq")
        B.i(gp.collective_compute("AllGather", ALU.bypass, replica_groups=[list(range(NCORE))],
                                  ins=[DR["lnw_in"].ap().opt()],
                                  outs=[ctx["V"]["lnw_out"].opt()]), "cc")
    blk.gpsimd(gcc)


def _scan_phase(ctx, call):
    nc, B, blk, S, DR, PS = (ctx[k] for k in ["nc", "B", "blk", "S", "DR", "PS"])

    def ve1(ve):
        B.i(ve.reduce_sum(S["cnt8"][0:64, 0:1], S["scanb"][0:64, :], axis=AX.X), "ve")
    blk.vector(ve1)

    def gq1(gp):
        B.w(gp, "ve")
        B.i(gp.dma_start(out=DR["cnt_in"].ap(), in_=S["cnt8"][0:64, 0:1]), "gq", 16)
        B.w(gp, "gq")
        B.i(gp.collective_compute("AllGather", ALU.bypass, replica_groups=[list(range(NCORE))],
                                  ins=[DR["cnt_in"].ap().opt()],
                                  outs=[ctx["V"]["cnt_out"].opt()]), "cc")
        B.w(gp, "cc")
        B.i(gp.dma_start(out=S["t32a"][0:8, 0:64],
                         in_=ctx["V"]["cnt_out"].rearrange("(k e) c -> k (e c)", e=64)), "gq", 16)
    blk.gpsimd(gq1)

    def mmc(mm):
        B.w(mm, "gq", "ve")
        B.i(nc.tensor.transpose(PS["psF"][0:64, 0:8], S["t32a"][0:8, 0:64],
                                S["identF"][0:8, 0:8]), "mm")
    blk.tensor(mmc)

    def vec8(ve):
        B.w(ve, "mm")
        B.i(ve.tensor_copy(S["cnt8"][0:64, 0:8], PS["psF"][0:64, 0:8]), "ve")
    blk.vector(vec8)

    def ve2(ve):
        B.w(ve, "gq")
        ve.memset(S["base"][0:64, :], 0.0)
        for k in range(NCORE):
            ve.scalar_tensor_tensor(S["base"][0:64, :], S["cnt8"][0:64, k:k + 1],
                                    S["maskp"][0:64, k:k + 1], S["base"][0:64, :],
                                    ALU.mult, ALU.add)
        ve.tensor_tensor_scan(S["scanb"][0:64, :], S["scanb"][0:64, :], S["scanb"][0:64, :],
                              0.0, ALU.add, ALU.bypass)
        B.i(ve.tensor_scalar(S["scanb"][0:64, :], S["scanb"][0:64, :], S["base"][0:64, 0:1],
                             None, ALU.add), "ve")
    blk.vector(ve2)


def _sigma_scatter(ctx, call):
    nc, B, blk, A, S, DR, PS = (ctx[k] for k in ["nc", "B", "blk", "A", "S", "DR", "PS"])

    for wc in range(16):
        def mm1(mm, wc=wc):
            B.w(mm, "ve")
            B.i(nc.tensor.transpose(PS["psF"][:, 0:64],
                                    S["scanb"][0:64, 128 * wc:128 * (wc + 1)],
                                    S["identF"][0:64, 0:64]), "mm")
        blk.tensor(mm1)

        def vsig(ve, wc=wc):
            B.w(ve, "mm", "gq")
            ve.tensor_copy(S["post"][:, :], PS["psF"][:, 0:64])
            for j in range(4):
                ve.tensor_scalar(S["t32a"][:, 0:64], S["logit"][:, wc, :],
                                 S["mx8"][:, wc, j:j + 1], None, ALU.is_equal)
                ve.tensor_tensor(S["t32b"][:, 0:64], S["post"][:, :], S["t32a"][:, 0:64],
                                 ALU.mult)
                ve.reduce_sum(S["t32c"][:, j:j + 1], S["t32b"][:, 0:64], axis=AX.X)
                ve.tensor_tensor(S["t32b"][:, 0:64], S["iotaE"][:, :], S["t32a"][:, 0:64],
                                 ALU.mult)
                ve.reduce_sum(S["t32d"][:, j:j + 1], S["t32b"][:, 0:64], axis=AX.X)
            ia, ib, ic = S["i32a"], S["i32b"], S["i32c"]
            ve.tensor_scalar(S["t32c"][:, 0:4], S["t32c"][:, 0:4], -1.0, None, ALU.add)
            ve.tensor_scalar(S["scat4"][:, 0, 0:1], S["tokb"][:, :], float(128 * wc),
                             None, ALU.add)
            ve.tensor_copy(ia[:, 0:4], S["t32d"][:, 0:4])
            ve.tensor_scalar(S["t32a"][:, 0:4], S["t32c"][:, 0:4], float(CAP), None,
                             ALU.is_ge)
            ve.tensor_copy(S["scat4"][:, 0, 1:2], S["wj4"][:, wc, 0:1])
            ve.tensor_copy(ib[:, 0:4], S["t32c"][:, 0:4])
            ve.tensor_scalar(ia[:, 4:8], ia[:, 0:4], 8, None,
                             ALU.logical_shift_left)
            ve.tensor_scalar(S["t32b"][:, 0:4], S["t32a"][:, 0:4], -1.0, 1.0,
                             ALU.mult, ALU.add)
            ve.tensor_scalar(ia[:, 8:12], ia[:, 0:4], 3, 4,
                             ALU.logical_shift_right, ALU.logical_shift_left)
            ve.tensor_copy(ic[:, 0:4], S["t32a"][:, 0:4])
            ve.tensor_tensor(ib[:, 4:8], ia[:, 4:8], ib[:, 0:4], ALU.add)
            ve.tensor_copy(ic[:, 4:8], S["t32b"][:, 0:4])
            ve.tensor_scalar(S["scat4"][:, 1, 0:1], S["tokb"][:, :], float(128 * wc),
                             None, ALU.add)
            ve.tensor_scalar(ic[:, 8:12], ic[:, 0:4], 11, None,
                             ALU.logical_shift_left)
            ve.tensor_scalar(ia[:, 12:16], ib[:, 4:8], 127, 7,
                             ALU.bitwise_and, ALU.logical_shift_left)
            ve.tensor_tensor(ia[:, 16:20], ia[:, 8:12], ib[:, 4:8], ALU.add)
            ve.tensor_scalar(ia[:, 20:24], ib[:, 4:8], 7, None,
                             ALU.logical_shift_right)
            ve.tensor_scalar(S["t32c"][:, 4:8], S["t32a"][:, 0:4], S["trashp"][:, 0:1],
                             None, ALU.mult)
            ve.tensor_tensor(ia[:, 24:28], ia[:, 16:20], ic[:, 4:8], ALU.mult)
            ve.tensor_tensor(ia[:, 28:32], ia[:, 12:16], ia[:, 20:24], ALU.add)
            ve.tensor_copy(S["scat4"][:, 1, 1:2], S["wj4"][:, wc, 1:2])
            ve.tensor_copy(ic[:, 12:16], S["t32c"][:, 4:8])
            ve.tensor_scalar(S["scat4"][:, 2, 0:1], S["tokb"][:, :], float(128 * wc),
                             None, ALU.add)
            ve.tensor_tensor(S["sig"][:, wc, :], ia[:, 24:28], ic[:, 8:12], ALU.add)
            ve.tensor_tensor(ia[:, 32:36], ia[:, 28:32], ic[:, 4:8], ALU.mult)
            ve.tensor_copy(S["scat4"][:, 2, 1:2], S["wj4"][:, wc, 2:3])
            ve.tensor_scalar(S["scat4"][:, 3, 0:1], S["tokb"][:, :], float(128 * wc),
                             None, ALU.add)
            ve.tensor_tensor(ib[:, 4:8], ia[:, 32:36], ic[:, 12:16], ALU.add)
            B.i(ve.tensor_copy(S["scat4"][:, 3, 1:2], S["wj4"][:, wc, 3:4]), "ve")
        blk.vector(vsig)

        for j in range(4):
            def gsc(gp, j=j):
                B.w(gp, "ve")
                B.i(gp.indirect_dma_start(DR["tbl_in"].ap(),
                                          IndirectOffsetOnAxis(S["i32b"][:, 4 + j:5 + j], 0),
                                          S["scat4"][:, j, :], None,
                                          compute_op=ALU.add), "gq", 16)
            blk.gpsimd(gsc)

    def gar(gp):
        B.w(gp, "gq")
        B.i(gp.collective_compute("AllGather", ALU.bypass, replica_groups=[list(range(NCORE))],
                                  ins=[DR["tbl_in"].ap().opt()],
                                  outs=[ctx["V"]["tbl_ag"].opt()]), "cc")
    blk.gpsimd(gar)

    def syt0(sy):
        B.w(sy, "cc")
        B.i(sy.dma_start(out=S["tbb"][:, 0, :], in_=ctx["V"]["tbl_rk"](0)), "ld", 16)
        B.i(sy.dma_start(out=S["tbb"][:, 1, :], in_=ctx["V"]["tbl_rk"](1)), "ld", 16)
    blk.sync(syt0)
    for r in range(2, NCORE + 1):
        def vetk(ve):
            B.w(ve, "ld")
            B.i(ve.tensor_tensor(S["tbb"][:, 0, :], S["tbb"][:, 0, :], S["tbb"][:, 1, :],
                                 ALU.add), "ve")
        blk.vector(vetk)
        if r < NCORE:
            def sytk(sy, r=r):
                B.w(sy, "ve")
                B.i(sy.dma_start(out=S["tbb"][:, 1, :], in_=ctx["V"]["tbl_rk"](r)), "ld", 16)
            blk.sync(sytk)

    def gsum(gp):
        B.w(gp, "ve")
        B.i(gp.dma_start(out=DR["tblsum"].ap().rearrange("(p a) c -> p (a c)", p=128),
                         in_=S["tbb"][:, 0, :]), "gq", 16)
        B.w(gp, "gq")
        B.i(gp.indirect_dma_start(S["tslice"][:, :], None, DR["tblsum"].ap(),
                                  IndirectOffsetOnAxis(S["slrows"][:, :], 0)), "gq", 16)
    blk.gpsimd(gsum)

    def vtw(ve):
        B.w(ve, "gq")
        tsv = S["tslice"][:, :].rearrange("p (n c) -> p n c", c=2)
        ve.tensor_copy(S["tidx"][:, :], tsv[:, :, 0])
        ve.tensor_scalar(S["twgt"][:, :], tsv[:, :, 1], 0.5, None, ALU.mult)
        ve.tensor_copy(S["t32a"][:, 0:16], tsv[:, :, 1])
        ve.tensor_scalar(S["i32a"][:, 0:16], S["tidx"][:, :], 11, 4,
                         ALU.logical_shift_right, ALU.logical_shift_left)
        ve.tensor_copy(S["t32b"][:, 0:16], S["t32a"][:, 0:16])
        ve.tensor_copy(S["t32c"][:, 0:16], S["t32a"][:, 0:16])
        B.i(ve.tensor_tensor(S["tidx"][:, :], S["tidx"][:, :], S["i32a"][:, 0:16],
                             ALU.add), "ve")
    blk.vector(vtw)


def _dbg_dump(ctx):
    nc, B, blk, S, DR, PS = (ctx[k] for k in ["nc", "B", "blk", "S", "DR", "PS"])
    dbg = ctx["dbg"]

    def gd(gp):
        B.w(gp, "ve", "gq", "cc", "mm", "sc")
        B.i(gp.dma_start(out=dbg[:, 0:32], in_=S["tslice"][:, :]), "gq", 16)
        B.i(gp.dma_start(out=dbg[:, 32:96].bitcast(I32).rearrange("p (a b) -> p a b", b=4),
                         in_=S["sig"][:, :, :]), "gq", 16)
        B.i(gp.dma_start(out=dbg[0:64, 96:104], in_=S["cnt8"][0:64, :]), "gq", 16)
        B.i(gp.dma_start(out=dbg[0:64, 112:120], in_=S["cnt8"][0:64, :]), "gq", 16)
        B.i(gp.dma_start(out=dbg[0:8, 120:184], in_=S["t32a"][0:8, 0:64]), "gq", 16)
        B.i(gp.dma_start(out=dbg[0:64, 184:192], in_=ctx["V"]["cnt_out"].rearrange("(k e) c -> e k c", e=64)[0:64, 0:8, 0], allow_slow_non_contiguous=True), "gq", 16)
        B.i(gp.dma_start(out=dbg[:, 128:192], in_=S["logit"][:, 0, :]), "gq", 16)
        B.i(gp.dma_start(out=dbg[:, 192:200], in_=S["mx8"][:, 0, :]), "gq", 16)
        B.i(gp.dma_start(out=dbg[0:64, 1024:3072], in_=S["scanb"][0:64, :]), "gq", 16)
        B.i(gp.dma_start(out=dbg[:, 3072:3584], in_=S["lnc0"][:, :]), "gq", 16)
        B.i(gp.dma_start(out=dbg[:, 232:1000].bitcast(BF16)[:, 0:1536], in_=S["wbuf"][:, 0:1536]), "gq", 16)
        B.i(gp.dma_start(out=dbg[:, 3584:4096].bitcast(BF16)[:, 0:1024],
                         in_=S["wbuf"][:, 8 * GUW:8 * GUW + 1024]), "gq", 16)
        B.i(gp.dma_start(out=dbg[:, 200:216], in_=S["twgt"][:, :]), "gq", 16)
        B.i(gp.dma_start(out=dbg[:, 216:232].bitcast(I32), in_=S["tidx"][:, :]), "gq", 16)
        B.w(gp, "gq")
    blk.gpsimd(gd)


def _experts(ctx, call):
    nc, B, blk, S, DR, PS = (ctx[k] for k in ["nc", "B", "blk", "S", "DR", "PS"])

    for e in range(EPC):
        if e == 0:
            def syw(sy):
                B.w(sy, "mm", "sq")
                B.i(sy.dma_start(out=S["wbuf"][:, :], in_=DR["wstage"].ap()[0]), "ld", 16)
            blk.sync(syw)

        for s in range(2):
            def gg(gp, e=e, s=s):
                B.w(gp, "ve")
                B.w(gp, "mm")
                B.i(gp.indirect_dma_start(S["gb"][:, s, :], None, ctx["V"]["lnw_out"],
                                          IndirectOffsetOnAxis(S["tidx"][:, 2 * e + s:2 * e + s + 1], 0)),
                    "gq", 16)
            blk.gpsimd(gg)

        for s in range(2):
            def mmt(mm, s=s):
                B.w(mm, "gq", "ve")
                for cc in range(8):
                    ins = nc.tensor.transpose(PS["ptrB"][:, 128 * cc:128 * (cc + 1)],
                                              S["gb"][:, s, 128 * cc:128 * (cc + 1)],
                                              S["identB"][:, :])
                B.i(ins, "mm")
            blk.tensor(mmt)

            def vet(ve, s=s):
                B.w(ve, "mm")
                B.i(ve.tensor_copy(S["gbT"][:, :, 128 * s:128 * (s + 1)],
                                   PS["ptrB"][:, :].rearrange("p (a q) -> p a q", a=8)), "ve")
            blk.vector(vet)

        for ip in range(6):
            def mmg(mm, ip=ip):
                B.w(mm, "ve", "sc", "ld")
                for t in range(8):
                    nc.tensor.matmul(PS["psA"][:, 0:256],
                                     S["wbuf"][:, GUW * t + 128 * ip:GUW * t + 128 * (ip + 1)],
                                     S["gbT"][:, t, :], start=(t == 0), stop=(t == 7))
                for t in range(8):
                    ins = nc.tensor.matmul(PS["psB"][:, 0:256],
                                           S["wbuf"][:, GUW * t + 128 * (ip + 6):GUW * t + 128 * (ip + 7)],
                                           S["gbT"][:, t, :], start=(t == 0), stop=(t == 7))
                B.i(ins, "mm")
            blk.tensor(mmg)

            def scg(sc):
                B.w(sc, "mm")
                B.i(sc.activation(out=S["tb16b"][:, 0:256], in_=PS["psA"][:, 0:256],
                                  func=AF.Silu), "sc")
            blk.scalar(scg)

            def veg(ve, ip=ip):
                B.w(ve, "sc")
                B.i(ve.tensor_tensor(S["h6"][:, ip, 0:256], S["tb16b"][:, 0:256],
                                     PS["psB"][:, 0:256], ALU.mult), "ve")
            blk.vector(veg)

        if e + 1 < EPC:
            def sygu(sy, e=e):
                B.w(sy, "mm", "sq")
                B.i(sy.dma_start(out=S["wbuf"][:, 0:8 * GUW],
                                 in_=DR["wstage"].ap()[e + 1][:, 0:8 * GUW]), "ld", 16)
            blk.sync(sygu)

        for s in range(2):
            for dh in range(2):
                def mmd(mm, s=s, dh=dh):
                    B.w(mm, "ve", "sc")
                    for ip in range(6):
                        ins = nc.tensor.matmul(
                            PS["psC"][:, :],
                            S["h6"][:, ip, 128 * s:128 * (s + 1)],
                            S["wbuf"][:, 8 * GUW + D * ip + 512 * dh:8 * GUW + D * ip + 512 * (dh + 1)],
                            start=(ip == 0), stop=(ip == 5))
                    B.i(ins, "mm")
                blk.tensor(mmd)

                def ved(ve, e=e, s=s, dh=dh):
                    B.w(ve, "mm", "gq")
                    B.i(ve.tensor_scalar(S["eo"][:, 512 * dh:512 * (dh + 1)], PS["psC"][:, :],
                                         S["twgt"][:, 2 * e + s:2 * e + s + 1], None,
                                         ALU.mult), "ve")
                blk.vector(ved)

            def geo(gp, e=e, s=s):
                B.w(gp, "ve", "cc")
                B.i(gp.dma_start(
                    out=DR["eo_in"].ap()[0:NSLOT, :].rearrange("(n p) d -> p n d", p=128)[:, 2 * e + s],
                    in_=S["eo"][:, :]), "gq", 16)
            blk.gpsimd(geo)

        if e + 1 < EPC:
            def sydn(sy, e=e):
                B.w(sy, "mm")
                B.i(sy.dma_start(out=S["wbuf"][:, 8 * GUW:WROW],
                                 in_=DR["wstage"].ap()[e + 1][:, 8 * GUW:WROW]), "ld", 16)
            blk.sync(sydn)


def _combine(ctx, call):
    nc, B, blk, S, DR, PS = (ctx[k] for k in ["nc", "B", "blk", "S", "DR", "PS"])

    def gcc(gp):
        B.w(gp, "gq")
        B.i(gp.collective_compute("AllGather", ALU.bypass, replica_groups=[list(range(NCORE))],
                                  ins=[DR["eo_in"].ap().opt()],
                                  outs=[ctx["V"]["eo_out"].opt()]), "cc")
        B.w(gp, "cc")
    blk.gpsimd(gcc)

    qk = S["qkv"][:, :, :].rearrange("p a m -> p (a m)")
    gq_after = {}

    def emit_g(wc):
        for j in range(4):
            def gx(gp, wc=wc, j=j):
                B.w(gp, "ve")
                s = 4 * (wc % 2) + j
                B.i(gp.indirect_dma_start(qk[:, 1024 * s:1024 * (s + 1)], None,
                                          ctx["V"]["gall"],
                                          IndirectOffsetOnAxis(S["sig"][:, wc, j:j + 1], 0),
                                          element_offset=ctx["V"]["eo_row0"] * D),
                    "gq", 16)
            blk.gpsimd(gx)
        gq_after[wc] = B.v["gq"]

    emit_g(0)
    for wc in range(16):
        if wc + 1 < 16:
            emit_g(wc + 1)

        def vsum(ve, wc=wc):
            ve.wait_ge(B.h["gq"], gq_after[wc])
            b = 4 * (wc % 2)
            ve.tensor_tensor(S["xs"][:, :], qk[:, 1024 * b:1024 * (b + 1)],
                             qk[:, 1024 * (b + 1):1024 * (b + 2)], ALU.add)
            ve.tensor_tensor(S["tb16b"][:, 0:D], qk[:, 1024 * (b + 2):1024 * (b + 3)],
                             qk[:, 1024 * (b + 3):1024 * (b + 4)], ALU.add)
            B.i(ve.tensor_tensor(S["xs"][:, :], S["xs"][:, :], S["tb16b"][:, 0:D],
                                 ALU.add), "ve")
        blk.vector(vsum)

        def mmx(mm):
            B.w(mm, "ve")
            for cc in range(8):
                ins = nc.tensor.transpose(PS["ptrA"][:, 128 * cc:128 * (cc + 1)],
                                          S["xs"][:, 128 * cc:128 * (cc + 1)],
                                          S["identB"][:, :])
            B.i(ins, "mm")
        blk.tensor(mmx)

        def vadd(ve, wc=wc):
            B.w(ve, "mm")
            for t in range(8):
                ins = ve.scalar_tensor_tensor(S[f"res{t}"][:, 128 * wc:128 * (wc + 1)],
                                              PS["ptrA"][:, 128 * t:128 * (t + 1)], 1.0,
                                              S[f"res{t}"][:, 128 * wc:128 * (wc + 1)],
                                              ALU.mult, ALU.add)
            B.i(ins, "ve")
        blk.vector(vadd)


def _p_ffn(ctx, call, interleave=None):
    nc, B, blk, S, DR = (ctx[k] for k in ["nc", "B", "blk", "S", "DR"])
    sub = ctx.get("ffnsub", 99)
    units = []

    def u_pre():
        def gz(gp):
            B.w(gp, "cc")
            B.w(gp, "ve")
            B.i(gp.memset(S["t32d"][:, 0:2 * (TBL_ROWS // 128)], 0.0), "gp")
            tv = DR["tbl_in"].ap().rearrange("(p a) c -> p a c", p=128)
            B.i(gp.dma_start(out=tv,
                             in_=S["t32d"][:, 0:2 * (TBL_ROWS // 128)].rearrange(
                                 "p (a c) -> p a c", c=2)), "gq", 16)
        blk.gpsimd(gz)

        if call == 1:
            def syw(sy):
                B.w(sy, "mm", "sq")
                B.i(sy.dma_start(out=S["wbuf"][:, :], in_=DR["shstage"].ap()), "ld", 16)
            blk.sync(syw)
    units.append(u_pre)

    def mk_ch(ch):
        def u_ch():
            _ln_chunk(ctx, ch, "res")
            if ch < 4 and sub >= 1:
                _stage_and_route(ctx, call, ch)
            if sub >= 2:
                _shared_chunk(ctx, ch)
        return u_ch
    for ch in range(4):
        units.append(mk_ch(ch))

    def u_sta():
        if sub >= 3:
            _stage_transpose_ag(ctx, call)
        if sub >= 4:
            _scan_phase(ctx, call)
    units.append(u_sta)

    def u_sig():
        if sub >= 5:
            _sigma_scatter(ctx, call)
            if ctx["dbg"] is not None and call == 0:
                _dbg_dump(ctx)
    units.append(u_sig)

    for ch in range(4, 8):
        units.append(mk_ch(ch))

    other = list(interleave) if interleave else []
    i = j = 0
    while i < len(units) or j < len(other):
        if i < len(units):
            units[i]()
            i += 1
        if j < len(other):
            other[j]()
            j += 1

    if sub >= 6:
        _experts(ctx, call)
    if sub >= 7:
        _combine(ctx, call)


def _p_attn(ctx):
    nc, B, blk, S, DR, PS = (ctx[k] for k in ["nc", "B", "blk", "S", "DR", "PS"])

    for seg in range(2):
        B0 = SEG * seg

        def sywA(sy):
            B.w(sy, "mm", "sq")
            B.i(sy.dma_start(out=S["wbuf"][:, 0:16384], in_=DR["attnw"].ap()[:, 0:16384]),
                "ld", 16)
        blk.sync(sywA)

        def vec(ve, B0=B0):
            B.w(ve, "ve")
            for t in range(8):
                src = S[f"res{t}"][:, :].rearrange("p (a s r) -> p a s r", a=2, r=4)[:, seg, :, 0]
                ins = ve.tensor_copy(S[f"lnc{t}"][:, :], src)
            B.i(ins, "ve")
        blk.vector(vec)
        _ln_chunk(ctx, 0, "lnc")

        for ec in range(16):
            def mmq(mm, ec=ec):
                B.w(mm, "ve", "sc", "ld")
                for t in range(8):
                    ins = nc.tensor.matmul(PS["psA"][:, :],
                                           S["wbuf"][:, 2048 * t + 128 * ec:2048 * t + 128 * (ec + 1)],
                                           S[f"lnc{t}"][:, :], start=(t == 0), stop=(t == 7))
                B.i(ins, "mm")
            blk.tensor(mmq)

            def scq(ve, ec=ec):
                B.w(ve, "mm")
                B.i(ve.tensor_scalar(S["qkv"][:, ec, :], PS["psA"][:, :],
                                     S["binS"][:, ec:ec + 1], None, ALU.add), "ve")
            blk.vector(scq)

        def sywB(sy):
            B.w(sy, "mm", "sc", "sq")
            B.i(sy.dma_start(out=S["wbuf"][:, 0:16384],
                             in_=DR["attnw"].ap()[:, 16384:2 * 16384]), "ld", 16)
        blk.sync(sywB)

        for ec in range(8):
            def mmv(mm, ec=ec):
                B.w(mm, "ve", "sc", "ld")
                for t in range(8):
                    ins = nc.tensor.matmul(PS["psA"][:, :],
                                           S["wbuf"][:, 2048 * t + 128 * ec:2048 * t + 128 * (ec + 1)],
                                           S[f"lnc{t}"][:, :], start=(t == 0), stop=(t == 7))
                B.i(ins, "mm")
            blk.tensor(mmv)

            def scv(ve, ec=ec):
                B.w(ve, "mm")
                B.i(ve.tensor_scalar(S["qkv"][:, 16 + ec, :], PS["psA"][:, :],
                                     S["binS"][:, 16 + ec:17 + ec], None, ALU.add), "ve")
            blk.vector(scv)

        for h in range(NH):
            r0 = 64 * (h % 2)
            qh = S["qkv"][r0:r0 + 64, h // 2, :]
            kh = S["qkv"][r0:r0 + 64, 8 + h // 2, :]
            vh = S["qkv"][r0:r0 + 64, 16 + h // 2, :]

            sbank = ["psC", "psD", "psE", "psF"]

            def mms(mm, kh=kh, qh=qh, vh=vh, r0=r0):
                B.w(mm, "sc", "ve")
                for kt in range(4):
                    nc.tensor.matmul(PS[sbank[kt]][:, 128 * kt:512],
                                     kh[:, 128 * kt:128 * (kt + 1)], qh[:, 128 * kt:512],
                                     start=True, stop=True)
                for kt in range(4):
                    ins = nc.tensor.transpose(PS["ptrB"][:, 64 * kt:64 * (kt + 1)],
                                              vh[:, 128 * kt:128 * (kt + 1)],
                                              S["identB"][r0:r0 + 64, r0:r0 + 64])
                B.i(ins, "mm")
            blk.tensor(mms)

            def scs(sc):
                B.w(sc, "mm")
                sc.activation(out=S["tb16b"][:, 0:512], in_=PS["psC"][:, 0:512],
                              func=AF.Exp, scale=0.125)
                sc.activation(out=S["tb16b"][:, 512:896], in_=PS["psD"][:, 128:512],
                              func=AF.Exp, scale=0.125)
                sc.activation(out=S["tb16"][:, 0:256], in_=PS["psE"][:, 256:512],
                              func=AF.Exp, scale=0.125)
                B.i(sc.activation(out=S["tb16"][:, 256:384], in_=PS["psF"][:, 384:512],
                                  func=AF.Exp, scale=0.125), "sc")
            blk.scalar(scs)

            def ves(ve):
                B.w(ve, "sc", "mm")
                ve.tensor_tensor(S["pt"][:, 0, :], S["tb16b"][:, 0:512],
                                 S["cmask"][:, 0, :], ALU.mult)
                ve.tensor_tensor(S["pt"][:, 1, 128:512], S["tb16b"][:, 512:896],
                                 S["cmask"][:, 1, 128:512], ALU.mult)
                ve.tensor_tensor(S["pt"][:, 2, 256:512], S["tb16"][:, 0:256],
                                 S["cmask"][:, 2, 256:512], ALU.mult)
                ve.tensor_tensor(S["pt"][:, 3, 384:512], S["tb16"][:, 256:384],
                                 S["cmask"][:, 3, 384:512], ALU.mult)
                for kt in range(4):
                    ins = ve.tensor_copy(S["vt"][:, kt, 0:64],
                                         PS["ptrB"][:, 64 * kt:64 * (kt + 1)])
                B.i(ins, "ve")
            blk.vector(ves)

            def mmpv(mm):
                B.w(mm, "ve", "sc")
                for qt in range(4):
                    for kt in range(qt + 1):
                        ins = nc.tensor.matmul(PS["psA"][:, 65 * qt:65 * qt + 65],
                                               S["pt"][:, kt, 128 * qt:128 * (qt + 1)],
                                               S["vt"][:, kt, :], start=(kt == 0),
                                               stop=(kt == qt))
                B.i(ins, "mm")
            blk.tensor(mmpv)

            def vepv(ve):
                B.w(ve, "mm")
                for qt in range(4):
                    ve.reciprocal(S["t32a"][:, qt:qt + 1], PS["psA"][:, 65 * qt + 64:65 * qt + 65])
                for qt in range(4):
                    ve.tensor_copy(S["otm"][:, qt, :], PS["psA"][:, 65 * qt:65 * qt + 64])
                for qt in range(4):
                    ins = ve.tensor_scalar(S["otm"][:, qt, :], S["otm"][:, qt, :],
                                           S["t32a"][:, qt:qt + 1], None, ALU.mult)
                B.i(ins, "ve")
            blk.vector(vepv)

            def mmo(mm):
                B.w(mm, "ve")
                for qt in range(4):
                    ins = nc.tensor.transpose(PS["ptrB"][0:64, 128 * qt:128 * (qt + 1)],
                                              S["otm"][:, qt, :], S["identB"][:, :])
                B.i(ins, "mm")
            blk.tensor(mmo)

            def veo(ve, h=h, r0=r0):
                B.w(ve, "mm")
                for qt in range(4):
                    ins = ve.tensor_copy(S["ofm"][r0:r0 + 64, h // 2, 128 * qt:128 * (qt + 1)],
                                         PS["ptrB"][0:64, 128 * qt:128 * (qt + 1)])
                B.i(ins, "ve")
            blk.vector(veo)

        for dt in range(8):
            def mmop(mm, dt=dt):
                B.w(mm, "ve", "sc")
                for ct in range(8):
                    ins = nc.tensor.matmul(PS["psF"][:, :],
                                           S["wbuf"][:, 2048 * ct + 1024 + 128 * dt:2048 * ct + 1024 + 128 * (dt + 1)],
                                           S["ofm"][:, ct, :], start=(ct == 0), stop=(ct == 7))
                B.i(ins, "mm")
            blk.tensor(mmop)

            def scop(ve, dt=dt):
                B.w(ve, "mm")
                B.i(ve.tensor_scalar(S["tb16b"][:, 0:512], PS["psF"][:, :],
                                     S["boutS"][:, dt:dt + 1], None, ALU.add), "ve")
            blk.vector(scop)

            def veop(ve, dt=dt, seg=seg):
                B.w(ve, "sc")
                dst = S[f"res{dt}"][:, :].rearrange("p (a s r) -> p a s r", a=2, r=4)[:, seg, :, 0]
                B.i(ve.tensor_tensor(dst, dst, S["tb16b"][:, 0:512], ALU.add), "ve")
            blk.vector(veop)


def _p_output(ctx):
    nc, B, blk, S, DR = (ctx[k] for k in ["nc", "B", "blk", "S", "DR"])
    out_w, out_t = ctx["out_w"], ctx["out_t"]
    fmfull = (DR["bounce"].ap().rearrange("tk d -> (tk d)")
              .rearrange("(d s) -> d s", d=D))

    def sq1(sc):
        B.w(sc, "ve", "ld")
        for t in range(8):
            B.i(sc.dma_start(out=fmfull[128 * t:128 * (t + 1), :], in_=S[f"res{t}"][:, :]),
                "sq", 16)
    blk.scalar(sq1)

    bufs = [S["eo"], S["xs"], S["tb16"], S["tb16b"]]
    ld_after = {}

    def emit_tr(w):
        def sy1(sy, w=w):
            B.w(sy, "sq", "gq")
            B.i(sy.dma_start_transpose(bufs[w % 4][:, :], fmfull[:, 128 * w:128 * (w + 1)]),
                "ld", 16)
        blk.sync(sy1)
        ld_after[w] = B.v["ld"]

    def emit_wr(w):
        def gq1(gp, w=w):
            gp.wait_ge(B.h["ld"], ld_after[w])
            dst = (out_w if w < 16 else out_t)
            ww = w % 16
            B.i(gp.dma_start(out=dst[128 * ww:128 * (ww + 1), :], in_=bufs[w % 4][:, :]),
                "gq", 16)
        blk.gpsimd(gq1)

    for w in range(4):
        emit_tr(w)
    for w in range(32):
        emit_wr(w)
        if w + 4 < 32:
            emit_tr(w + 4)

    def fin(gp):
        B.w(gp, "gq", "ld", "sq")
        gp.nop()
    blk.gpsimd(fin)


def kernel(**inputs):
    maps = _host_prep(inputs)
    nc = _build()
    res = run_bass_kernel_spmd(nc, maps, core_ids=list(range(NCORE)))
    out = np.zeros((8, 2, SEG, D), np.float32)
    for c in range(NCORE):
        out[c // 2, c % 2] = res.results[c]["out_w"]
        out[c // 2 + 4, c % 2] = res.results[c]["out_t"]
    return np.ascontiguousarray(out.reshape(8, 4096, D)[:, :, PD])
